# version 03
# baseline (speedup 1.0000x reference)
import math
import contextlib
import numpy as np
import ml_dtypes
import concourse.bass as bass
import concourse.mybir as mybir
from concourse.bass_utils import run_bass_kernel_spmd
from concourse.bass import IndirectOffsetOnAxis

F32 = mybir.dt.float32
BF16 = mybir.dt.bfloat16
U32 = mybir.dt.uint32
AF = mybir.ActivationFunctionType
ALU = mybir.AluOpType
AX = mybir.AxisListType
BF = ml_dtypes.bfloat16

NC = 8
GRP = 4
HD = 128
TOKG = 512
LN_EPS = 1e-5
LORA_SCALE = 4.0


class CFG:
    def __init__(self, D=4096, S=4096):
        self.D, self.S = D, S
        self.F = D // 128
        self.H = D // HD
        self.HPC = self.H // GRP
        self.WC = self.HPC * HD
        self.NTG = S // TOKG
        self.NQG = S // TOKG
        self.OWN = S // GRP
        self.NOT = self.OWN // 128
        self.NOG = self.OWN // TOKG
        self.MODSH = 6 * D // NC
        self.GASH = D // GRP
        self.BB = 512 if self.MODSH % 512 == 0 else 128
        self.GC = min(512, self.GASH)
        assert S % TOKG == 0 and D % 512 == 0 and self.OWN % TOKG == 0
        assert self.MODSH % self.BB == 0 and self.GASH % self.GC == 0
        assert self.F % 4 == 0 and self.HPC >= 1


class Prog:
    ENGINES = ("t", "v", "sc", "sy", "g")

    CSEM = {"t": "tC", "v": "vC", "sc": "scC", "sy": "syC", "g": "gC"}

    def __init__(self):
        self.ops = {e: [] for e in self.ENGINES}
        self.cnt = {}
        self.kind = {}
        self.owner = {}
        self.last_ev = {}

    def op(self, eng, fn, waits=(), inc=None, amt=1, kind="c"):
        ev = None
        if inc is not None:
            sem = self.CSEM[eng] if kind == "c" else inc
            if sem in self.owner:
                assert self.owner[sem] == eng and self.kind[sem] == kind, sem
            self.owner[sem] = eng
            self.kind[sem] = kind
            self.cnt[sem] = self.cnt.get(sem, 0) + amt
            ev = (sem, self.cnt[sem])
            self.last_ev[inc] = ev
        ws = tuple(w for w in waits if w is not None and w[1] > 0)
        self.ops[eng].append((ws, fn, ev[0] if ev else None, amt))
        return ev

    def dma(self, eng, fn, waits=(), inc=None):
        return self.op(eng, fn, waits, inc, 16, "d")

    def ev(self, name):
        if name in self.cnt:
            return (name, self.cnt[name])
        return self.last_ev.get(name)

    def emit(self, nc, ctx):
        sems = {}
        for name in self.cnt:
            sems[name] = ctx.enter_context(nc.semaphore(name))
        block = ctx.enter_context(nc.Block())

        def section(eng_name):
            def body(eng):
                seen = {}
                for ws, fn, inc, amt in self.ops[eng_name]:
                    for sem, val in ws:
                        if seen.get(sem, -1) >= val:
                            continue
                        eng.wait_ge(sems[sem], val)
                        seen[sem] = val
                    if fn is None:
                        continue
                    ins = fn(eng)
                    if inc is not None:
                        ins.then_inc(sems[inc], amt)
            return body

        block.tensor(section("t"))
        block.vector(section("v"))
        block.scalar(section("sc"))
        block.sync(section("sy"))
        block.gpsimd(section("g"))


def host_constants(cfg):
    D, S = cfg.D, cfg.S
    c = {}
    c["ident"] = np.eye(128, dtype=BF)
    c["identf"] = np.eye(128, dtype=np.float32)
    c["ones128"] = np.ones((128, 128), dtype=BF)
    c["onesrow"] = np.ones((1, 128), dtype=np.float32)
    P = np.zeros((128, 128), dtype=np.float32)
    for p in range(64):
        P[p + 64, p] = -1.0
        P[p, p + 64] = 1.0
    c["rotperm"] = P.astype(BF)
    inv = 1.0 / 10000.0 ** (np.arange(0, HD, 2, dtype=np.float64) / HD)
    invf = np.concatenate([inv, inv])
    ang = invf[:, None] * np.arange(S, dtype=np.float64)[None, :]
    c["cosT"] = np.cos(ang).astype(BF)
    c["sinT"] = np.sin(ang).astype(BF)
    nj = TOKG // 128
    m = np.zeros((128, nj * TOKG), dtype=BF)
    for j in range(nj):
        kp = np.arange(128)[:, None] + 128 * j
        qf = np.arange(TOKG)[None, :]
        m[:, j * TOKG:(j + 1) * TOKG] = (kp <= qf).astype(BF)
    c["masks"] = m
    c["expand"] = np.repeat(np.eye(4, dtype=np.float32), 16, axis=1)
    c["iota"] = np.arange(128, dtype=np.float32).reshape(128, 1)
    return c


def host_core_inputs(cfg, core, T, consts):
    D = cfg.D
    b, p = core // GRP, core % GRP
    x = np.ascontiguousarray(T["x"][b])
    wsl = slice(cfg.WC * p, cfg.WC * (p + 1))
    d = dict(consts)
    d["xb"] = x
    d["xown"] = np.ascontiguousarray(x[cfg.OWN * p: cfg.OWN * (p + 1)])
    d["wq"] = np.ascontiguousarray(T["wq"][:, wsl])
    d["wk"] = np.ascontiguousarray(T["wk"][:, wsl])
    d["wv"] = np.ascontiguousarray(T["wv"][:, wsl])
    d["wo"] = T["wo"]
    d["wmod_s"] = np.ascontiguousarray(
        T["wmod"][:, cfg.MODSH * core: cfg.MODSH * (core + 1)])
    gbase = (2 * D if core < GRP else 5 * D) + cfg.GASH * (core % GRP)
    d["wmod_g"] = np.ascontiguousarray(T["wmod"][:, gbase: gbase + cfg.GASH])
    d["cT"] = np.ascontiguousarray(T["c"].T)
    d["routerT"] = np.ascontiguousarray(T["router_w"].T)
    d["Apk"] = np.ascontiguousarray(T["A"].transpose(2, 0, 1).reshape(D, 64))
    d["Bpk"] = np.ascontiguousarray(T["B"].transpose(0, 2, 1).reshape(64, D))
    bs = np.zeros((2 * NC, NC * 128), dtype=np.float32)
    for r in range(NC):
        bs[2 * r + b, 128 * r: 128 * (r + 1)] = 1.0
    d["bsel16"] = bs
    d["baseq"] = np.stack(
        [np.full(128, float((cfg.NOG * p + k) * D), dtype=np.float32)
         for k in range(cfg.NOG)], axis=1)
    return d


def build(cfg):
    D, S, F, HPC, WC = cfg.D, cfg.S, cfg.F, cfg.HPC, cfg.WC
    NTG, NQG, OWN, NOG = cfg.NTG, cfg.NQG, cfg.OWN, cfg.NOG
    NJ = TOKG // 128
    ND512 = D // 512
    nc = bass.Bass("TRN2", target_bir_lowering=False, debug=False)
    ctx = contextlib.ExitStack()

    def din(name, shape, dt=F32):
        return nc.dram_tensor(name, shape, dt, kind="ExternalInput").ap()

    xb = din("xb", [S, D]); xown = din("xown", [OWN, D])
    wq = din("wq", [D, WC]); wk = din("wk", [D, WC]); wv = din("wv", [D, WC])
    wo = din("wo", [D, D])
    wmod_s = din("wmod_s", [D, cfg.MODSH]); wmod_g = din("wmod_g", [D, cfg.GASH])
    cT = din("cT", [D, 2])
    routerT = din("routerT", [D, 4]); Apk = din("Apk", [D, 64]); Bpk = din("Bpk", [64, D])
    ident = din("ident", [128, 128], BF16); identf = din("identf", [128, 128], F32)
    ones128 = din("ones128", [128, 128], BF16); onesrow = din("onesrow", [1, 128], F32)
    rotperm = din("rotperm", [128, 128], BF16)
    cosT = din("cosT", [128, S], BF16); sinT = din("sinT", [128, S], BF16)
    masks = din("masks", [128, NJ * TOKG], BF16)
    expand = din("expand", [4, 64], F32); iota = din("iota", [128, 1], F32)
    bsel16 = din("bsel16", [2 * NC, NC * 128], F32)
    baseq = din("baseq", [128, NOG], F32)
    OUT = nc.dram_tensor("out", [OWN, D], F32, kind="ExternalOutput").ap()

    h_fm = nc.dram_tensor("h_fm", [F, 128, S], BF16).ap()
    q_sp = nc.dram_tensor("q_sp", [HPC, 128, S], BF16).ap()
    k_sp = nc.dram_tensor("k_sp", [HPC, 128, S], BF16).ap()
    v_sp = nc.dram_tensor("v_sp", [HPC, 128, S], BF16).ap()
    x2_hb = nc.dram_tensor("x2_hb", [OWN, D], BF16).ap()
    agm_i = nc.dram_tensor("agm_i", [2, cfg.MODSH], F32)
    agm_o = nc.dram_tensor("agm_o", [2 * NC, cfg.MODSH], F32)
    agg_i = nc.dram_tensor("agg_i", [2, cfg.GASH], F32)
    agg_o = nc.dram_tensor("agg_o", [2 * NC, cfg.GASH], F32)
    aga_i = nc.dram_tensor("aga_i", [NQG, WC, TOKG], BF16)
    aga_o = nc.dram_tensor("aga_o", [NQG, GRP * WC, TOKG], BF16)

    pA = ctx.enter_context(nc.psum_tensor("pA", [128, 512], F32))
    pB = ctx.enter_context(nc.psum_tensor("pB", [128, 512], F32))
    pC = ctx.enter_context(nc.psum_tensor("pC", [128, 512], F32))
    pD = ctx.enter_context(nc.psum_tensor("pD", [128, 512], F32))
    pE = ctx.enter_context(nc.psum_tensor("pE", [128, 512], F32))
    pF = ctx.enter_context(nc.psum_tensor("pF", [128, 512], F32))
    pM = ctx.enter_context(nc.psum_tensor("pM", [128, 512], F32))
    pT = ctx.enter_context(nc.psum_tensor("pT", [128, 512], BF16))
    banksAB = [pA, pB]
    banks6 = [pA, pB, pC, pD, pE, pF]

    sb = lambda n, s, d: ctx.enter_context(nc.sbuf_tensor(n, s, d))
    identS = sb("identS", [128, 128], BF16)
    identfS = sb("identfS", [128, 128], F32)
    ones128S = sb("ones128S", [128, 128], BF16)
    onesrowS = sb("onesrowS", [1, 128], F32)
    rotpermS = sb("rotpermS", [128, 128], BF16)
    masksS = sb("masksS", [128, NJ * TOKG], BF16)
    expandS = sb("expandS", [4, 64], F32)
    iotaS = sb("iotaS", [128, 1], F32)
    bselS = sb("bselS", [2 * NC, NC * 128], F32)
    baseqS = sb("baseqS", [128, NOG], F32)
    cosb = sb("cosb", [128, 512], BF16)
    sinb = sb("sinb", [128, 512], BF16)
    capb = sb("capb", [128, D], BF16)
    sab = sb("sab", [128, D], BF16)
    gacol = sb("gacol", [128, max(cfg.F, 2)], F32)
    gb4col = sb("gb4col", [128, max(cfg.F, 2)], F32)
    negC = sb("negC", [128, 1], F32)
    idxb = sb("idxb", [128, max(NOG * F, 4)], U32)
    cts = sb("cts", [128, 2 * F], F32)
    ctsb = sb("ctsb", [128, 2 * F], BF16)
    ctsf = sb("ctsf", [128, 2 * F], F32)
    stats = sb("stats", [128, 6 * ND512], F32)
    mv = sb("mv", [128, 8], F32)
    tiny = sb("tiny", [128, 16], F32)
    sm = sb("sm", [128, 512], F32)
    sm2 = sb("sm2", [128, max(cfg.F, 2)], F32)
    modsh = sb("modsh", [2 * NC, cfg.MODSH], F32)
    modg = sb("modg", [2 * NC, cfg.GASH], F32)
    lgt = sb("lgt", [128, 16], F32)
    gat = sb("gat", [4, 128], F32)

    ARN = 64512
    AR = sb("AR", [128, ARN], BF16)
    ARFN = 8192
    ARF = sb("ARF", [128, ARFN], F32)

    def arv(off, n):
        assert off + n <= ARN, (off, n)
        return AR[:, off:off + n]

    def arf(off, n):
        assert off + n <= ARFN, (off, n)
        return ARF[:, off:off + n]

    WSLAB = 0
    HTG = F * WC
    S2STG = HTG + F * 512
    HTMS1 = S2STG + 4096
    S1STG = HTMS1 + D
    WMODBUF = S1STG + 1024
    assert WMODBUF + 2 * cfg.MODSH <= ARN
    KBUF = 0
    VBUF = 2 * S
    QBUF = 4 * S
    PRB = QBUF + 1024
    ATTS = PRB + 1536
    ATTG = 0
    WOSL = F * 512
    X2STG = WOSL + 2 * F * 512
    H2C = F * 512
    FFNW = H2C + F * 128
    BPK = FFNW + F * 68
    MIDB = BPK + D
    X2T = MIDB + 512
    H2R = X2T + 2 * D
    assert H2R + D <= ARN
    BpkS = AR[:, BPK:BPK + D]

    XT = 0
    GAB = 0
    WMG = D
    XS = D + 2048
    OUTS = D
    assert max(2 * D, D + 2048 + 1024, GAB + D, OUTS + D) <= ARFN

    P = Prog()
    E = {}
    RES = {"pM": None}

    for dst, src in ((identS, ident), (identfS, identf), (ones128S, ones128),
                     (onesrowS, onesrow), (rotpermS, rotperm),
                     (masksS, masks), (expandS, expand),
                     (iotaS, iota), (bselS, bsel16), (baseqS, baseq)):
        P.dma("sy", lambda e, d=dst, s=src: e.dma_start(out=d[:, :], in_=s), inc="cload")
    ev_cload = P.ev("cload")

    for k in range(NOG):
        for fi in range(F):
            P.op("v", lambda e, k=k, fi=fi: e.tensor_scalar(
                idxb[:, k * F + fi: k * F + fi + 1], iotaS[:, :],
                baseqS[:, k:k + 1], float(fi * 128), ALU.add, ALU.add),
                waits=(ev_cload,), inc="idx")
    ev_idx = P.ev("idx")

    P.dma("sy", lambda e: e.dma_start(
        out=cts[:, :].rearrange("p (k b) -> p k b", k=F),
        in_=cT.rearrange("(k p) b -> p k b", p=128)), inc="ctl")
    ev_silu = P.op("sc", lambda e: e.activation(ctsb[:, :], cts[:, :], AF.Silu),
                   waits=(P.ev("ctl"),), inc="silu")
    ev_siluf = P.op("sc", lambda e: e.activation(ctsf[:, :], cts[:, :], AF.Silu),
                    waits=(P.ev("ctl"),), inc="siluf")

    NCG = cfg.MODSH // cfg.BB
    ev_modmm = {}
    for ki in range(F):
        par = ki % 2
        off = WMODBUF + par * cfg.MODSH
        ev_l = P.dma("g", lambda e, ki=ki, off=off: e.dma_start(
            out=arv(off, cfg.MODSH), in_=wmod_s[128 * ki:128 * (ki + 1), :]),
            waits=(ev_modmm.get(ki - 2),), inc=f"wml{par}")
        for cg in range(NCG):
            bk = banks6[cg % 6]
            ev = P.op("t", lambda e, ki=ki, cg=cg, off=off, bk=bk: e.matmul(
                bk[0:2, 0:cfg.BB], ctsb[:, 2 * ki:2 * ki + 2],
                arv(off, cfg.MODSH)[:, cfg.BB * cg:cfg.BB * (cg + 1)],
                start=(ki == 0), stop=(ki == F - 1)),
                waits=(ev_silu, ev_l), inc="modmm" if cg == NCG - 1 else None)
        ev_modmm[ki] = ev
    ev_modev = None
    for cg in range(NCG):
        bk = banks6[cg % 6]
        ev_modev = P.op("v", lambda e, cg=cg, bk=bk: e.tensor_copy(
            modsh[0:2, cfg.BB * cg:cfg.BB * (cg + 1)], bk[0:2, 0:cfg.BB]),
            waits=(ev_modmm[F - 1],), inc="modev")
    ev_modst = P.dma("sy", lambda e: e.dma_start(out=agm_i.ap(), in_=modsh[0:2, :]),
                     waits=(ev_modev,), inc="modst")
    ev_agm = P.op("g", lambda e: e.collective_compute(
        "AllGather", ALU.bypass, replica_groups=[list(range(NC))],
        ins=[agm_i.ap().opt()], outs=[agm_o.ap().opt()]),
        waits=(ev_modst,), inc="agm", amt=1, kind="d")
    ev_modrb = P.dma("sy", lambda e: e.dma_start(out=modsh[:, :], in_=agm_o.ap()),
                     waits=(ev_agm,), inc="modrb")

    def bcast_gate(gate_idx, dst, add1, extra_wait=None):
        prev = None
        nb = D // cfg.BB
        for j in range(nb):
            col = gate_idx * D + cfg.BB * j
            r, loc = col // cfg.MODSH, col % cfg.MODSH
            evm = P.op("t", lambda e, r=r, loc=loc: e.matmul(
                pM[:, 0:cfg.BB], bselS[:, 128 * r:128 * (r + 1)],
                modsh[:, loc:loc + cfg.BB], start=True, stop=True),
                waits=(ev_modrb, ev_cload, RES["pM"], prev, extra_wait), inc="bgm")
            if add1:
                prev = P.op("v", lambda e, j=j, dst=dst: e.tensor_scalar(
                    dst[:, cfg.BB * j:cfg.BB * (j + 1)], pM[:, 0:cfg.BB],
                    1.0, None, ALU.add), waits=(evm,), inc="bgv")
            else:
                prev = P.op("v", lambda e, j=j, dst=dst: e.tensor_copy(
                    dst[:, cfg.BB * j:cfg.BB * (j + 1)], pM[:, 0:cfg.BB]),
                    waits=(evm,), inc="bgv")
            RES["pM"] = prev
        return prev

    ev_capb = bcast_gate(1, capb, True)
    ev_sab = bcast_gate(0, sab, False)

    NXT = S // 128
    ev_s1done = {}
    ev_ln_prev = None
    ev_ht_free = None
    ev_hw = {}
    ev_pt_free = None
    ev_hst = [None, None]
    ev_s1_last = None
    for xt in range(NXT):
        par = xt % 2
        tg = (xt * 128) // TOKG
        ev_xl = P.dma("sy", lambda e, xt=xt, par=par: e.dma_start(
            out=arf(XT + par * D, D), in_=xb[128 * xt:128 * (xt + 1), :]),
            waits=(ev_s1done.get(xt - 2),), inc=f"xld{par}")
        ev_bs = None
        for i in range(ND512):
            ev_bs = P.op("v", lambda e, i=i, par=par: e.bn_stats(
                stats[:, 6 * i:6 * i + 6],
                arf(XT + par * D, D)[:, 512 * i:512 * (i + 1)]),
                waits=(ev_xl, ev_ln_prev if i == 0 else None), inc="s1v")
        ev_ag = P.op("v", lambda e: e.bn_aggr(mv[:, 0:2], stats[:, :]),
                     waits=(ev_bs,), inc="s1v")
        ev_ep = P.op("v", lambda e: e.tensor_scalar(mv[:, 2:3], mv[:, 1:2], LN_EPS, None, ALU.add),
                     waits=(ev_ag,), inc="s1v")
        ev_v = P.op("v", lambda e: e.tensor_scalar(mv[:, 3:4], mv[:, 0:1], -1.0, None, ALU.mult),
                    waits=(ev_ag,), inc="s1v")
        ev_sq = P.op("sc", lambda e: e.activation(mv[:, 4:5], mv[:, 2:3], AF.Sqrt),
                     waits=(ev_v, ev_ep), inc="s1sq")
        ev_r = P.op("v", lambda e: e.reciprocal(mv[:, 5:6], mv[:, 4:5]), waits=(ev_sq,), inc="s1v")
        ev1 = P.op("v", lambda e, par=par: e.tensor_scalar(
            arv(HTMS1, D), arf(XT + par * D, D), mv[:, 3:4], mv[:, 5:6],
            ALU.add, ALU.mult), waits=(ev_r, ev_ht_free), inc="s1v")
        ev_ln_prev = ev1
        ev2 = P.op("v", lambda e: e.tensor_tensor(arv(HTMS1, D), arv(HTMS1, D), capb[:, :], ALU.mult),
                   waits=(ev1, ev_capb), inc="s1v")
        ev3 = P.op("v", lambda e: e.tensor_tensor(arv(HTMS1, D), arv(HTMS1, D), sab[:, :], ALU.add),
                   waits=(ev2, ev_sab), inc="s1v")
        ev_s1done[xt] = ev3
        ev_s1_last = ev3
        for g4 in range(F // 4):
            evt = None
            for j in range(4):
                fi = g4 * 4 + j
                evt = P.op("t", lambda e, fi=fi, j=j: e.transpose(
                    pT[:, 128 * j:128 * (j + 1)],
                    arv(HTMS1, D)[:, 128 * fi:128 * (fi + 1)], identS[:, :]),
                    waits=(ev3, ev_cload, ev_pt_free), inc="s1tr" if j == 3 else None)
            ev_ht_free = evt
            sl = g4 % 2
            evc = P.op("v", lambda e, sl=sl: e.tensor_copy(arv(S1STG + sl * 512, 512), pT[:, :]),
                       waits=(evt, ev_hst[sl]), inc="s1hc")
            ev_pt_free = evc
            ev_hst[sl] = P.dma("sy", lambda e, xt=xt, g4=g4, sl=sl: e.dma_start(
                out=h_fm[4 * g4:4 * g4 + 4, :, 128 * xt:128 * (xt + 1)].rearrange("f p t -> p f t"),
                in_=arv(S1STG + sl * 512, 512).rearrange("p (f t) -> p f t", f=4)),
                waits=(evc,), inc=f"hw{tg}")
    for tg in range(NTG):
        ev_hw[tg] = P.ev(f"hw{tg}")

    HF = F // 2
    ev_modev_all = P.ev("modev")

    def wslab_load(Wap, half, tag, wait_evs):
        ev = None
        for fi in range(HF):
            f = half * HF + fi
            ev = P.dma("g", lambda e, f=f, fi=fi, half=half, Wap=Wap: e.dma_start(
                out=arv(WSLAB + half * HF * WC + fi * WC, WC),
                in_=Wap[128 * f:128 * (f + 1), :]),
                waits=wait_evs, inc=f"wsl{tag}{half}")
        return ev

    P.op("v", lambda e: e.memset(tiny[0:1, 0:2], 0.0), inc="nrm")
    st_rot = {}
    ev_nrm_last = None
    ev_spill = {"q": None, "k": None, "v": None}
    ev_Wmm = {}

    def do_pass(W, spill, rot, first_mm_extra=()):
        nonlocal ev_nrm_last
        ni = 0 if W == "q" else 1
        ev_htg_prev_consumed = None
        for tg in range(NTG):
            tok0 = TOKG * tg
            ev_h = None
            for fi in range(F):
                ev_h = P.dma("sy", lambda e, fi=fi, tg=tg: e.dma_start(
                    out=arv(HTG + fi * 512, 512),
                    in_=h_fm[fi, :, tok0 + 0:tok0 + 512] if False else
                        h_fm[fi, :, TOKG * tg:TOKG * (tg + 1)]),
                    waits=(ev_hw[tg], ev_htg_prev_consumed), inc="htg")
            if rot:
                P.dma("sy", lambda e, tok0=tok0: e.dma_start(
                    out=cosb[:, :], in_=cosT[:, tok0:tok0 + 512]),
                    waits=(st_rot.get("cos_free"),), inc="cosl")
                P.dma("sy", lambda e, tok0=tok0: e.dma_start(
                    out=sinb[:, :], in_=sinT[:, tok0:tok0 + 512]),
                    waits=(st_rot.get("cos_free"),), inc="cosl")
                ev_cos = P.ev("cosl")
            ev_last_mm = None
            for cc in range(HPC):
                it = tg * HPC + cc
                par = it % 2
                bk = banksAB[par]
                SS = S2STG + 2048 * par
                extra = first_mm_extra if (tg * HPC + cc) < 2 else ()
                for f in range(F):
                    half = f // HF
                    ev_last_mm = P.op("t", lambda e, f=f, cc=cc, bk=bk, half=half: e.matmul(
                        bk[:, :], arv(WSLAB + half * HF * WC + (f % HF) * WC + cc * 128, 128),
                        arv(HTG + f * 512, 512),
                        start=(f == 0), stop=(f == F - 1)),
                        waits=(ev_h, E[f"wsl{W}{half}"],
                               st_rot.get(("evac", par))) + tuple(extra),
                        inc=f"{W}mm" if f == F - 1 else None)
                evm = ev_last_mm
                if rot:
                    e0 = P.op("v", lambda e, bk=bk, SS=SS: e.tensor_copy(arv(SS, 512), bk[:, :]),
                              waits=(evm, st_rot.get(("rd", par))), inc="s2v")
                    e1 = P.op("v", lambda e, SS=SS: e.tensor_tensor(
                        arv(SS + 512, 512), arv(SS, 512), arv(SS, 512), ALU.mult),
                        waits=(e0,), inc="s2v")
                    e2 = P.op("t", lambda e, SS=SS: e.matmul(
                        pM[0:1, :], ones128S[:, 0:1], arv(SS + 512, 512),
                        start=True, stop=True),
                        waits=(e1, RES["pM"]), inc="s2nt")
                    e3 = P.op("v", lambda e: e.reduce_max(tiny[0:1, 4:5], pM[0:1, :], axis=AX.X),
                              waits=(e2, ev_nrm_last), inc="s2v")
                    e4 = P.op("v", lambda e, ni=ni: e.tensor_tensor(
                        tiny[0:1, ni:ni + 1], tiny[0:1, ni:ni + 1], tiny[0:1, 4:5], ALU.max),
                        waits=(e3,), inc="nrm")
                    ev_nrm_last = e4
                    e5 = P.op("t", lambda e, SS=SS: e.matmul(
                        pM[:, :], rotpermS[:, :], arv(SS, 512), start=True, stop=True),
                        waits=(e0, ev_cload, e3), inc="s2rt")
                    e6 = P.op("v", lambda e, SS=SS: e.tensor_tensor(
                        arv(SS + 1024, 512), arv(SS, 512), cosb[:, :], ALU.mult),
                        waits=(e0, ev_cos, st_rot.get(("wr", par))), inc="s2v")
                    e7 = P.op("v", lambda e, SS=SS: e.tensor_tensor(
                        arv(SS + 1536, 512), pM[:, :], sinb[:, :], ALU.mult),
                        waits=(e5, ev_cos), inc="s2v")
                    RES["pM"] = e7
                    e8 = P.op("v", lambda e, SS=SS: e.tensor_tensor(
                        arv(SS + 1024, 512), arv(SS + 1024, 512), arv(SS + 1536, 512), ALU.add),
                        waits=(e6, e7, st_rot.get(("wr", par))), inc="s2v")
                    st_rot[("rd", par)] = e8
                    st_rot[("evac", par)] = e0
                    ev_sp = P.dma("sy", lambda e, cc=cc, tok0=tok0, spill=spill, SS=SS: e.dma_start(
                        out=spill[cc, :, tok0:tok0 + 512], in_=arv(SS + 1024, 512)),
                        waits=(e8,), inc=f"spw{par}")
                    st_rot[("wr", par)] = ev_sp
                    if cc == HPC - 1:
                        st_rot["cos_free"] = e8
                    ev_spill[W] = ev_sp
                else:
                    e0 = P.op("v", lambda e, bk=bk, SS=SS: e.tensor_copy(arv(SS, 512), bk[:, :]),
                              waits=(evm, st_rot.get(("rd", par))), inc="s2v")
                    evt = None
                    for j in range(4):
                        evt = P.op("t", lambda e, j=j, SS=SS: e.transpose(
                            pT[:, 128 * j:128 * (j + 1)],
                            arv(SS, 512)[:, 128 * j:128 * (j + 1)], identS[:, :]),
                            waits=(e0, ev_cload, st_rot.get("ptv")),
                            inc="s2tr" if j == 3 else None)
                    st_rot[("rd", par)] = evt
                    e1 = P.op("v", lambda e, SS=SS: e.tensor_copy(arv(SS + 1024, 512), pT[:, :]),
                              waits=(evt, st_rot.get(("wr", par))), inc="s2v")
                    st_rot["ptv"] = e1
                    ev_sp = P.dma("sy", lambda e, cc=cc, tok0=tok0, SS=SS: e.dma_start(
                        out=v_sp[cc, :, tok0:tok0 + 512], in_=arv(SS + 1024, 512)),
                        waits=(e1,), inc=f"spw{par}")
                    st_rot[("wr", par)] = ev_sp
                    ev_spill[W] = ev_sp
            ev_htg_prev_consumed = ev_last_mm
        ev_Wmm[W] = P.ev(f"{W}mm")

    E["wslq0"] = wslab_load(wq, 0, "q", ())
    E["wslq1"] = wslab_load(wq, 1, "q", ())
    do_pass("q", q_sp, True, first_mm_extra=(ev_modev,))
    E["wslk0"] = wslab_load(wk, 0, "k", (ev_Wmm["q"],))
    E["wslk1"] = wslab_load(wk, 1, "k", (ev_Wmm["q"],))
    do_pass("k", k_sp, True)
    E["wslv0"] = wslab_load(wv, 0, "v", (ev_Wmm["k"],))
    E["wslv1"] = wslab_load(wv, 1, "v", (ev_Wmm["k"],))
    st_rot["ptv"] = ev_pt_free
    do_pass("v", v_sp, False)
    ev_spw0, ev_spw1 = P.ev("spw0"), P.ev("spw1")

    ev = P.op("v", lambda e: e.tensor_tensor(tiny[0:1, 5:6], tiny[0:1, 0:1], tiny[0:1, 1:2], ALU.mult),
              waits=(ev_nrm_last,), inc="nrm")
    ev = P.op("v", lambda e: e.tensor_scalar(tiny[0:1, 6:7], tiny[0:1, 5:6], (1.02 ** 2) / HD, None, ALU.mult),
              waits=(ev,), inc="nrm")
    ev = P.op("sc", lambda e: e.activation(tiny[0:1, 7:8], tiny[0:1, 6:7], AF.Sqrt), waits=(ev,), inc="cbq")
    ev = P.op("v", lambda e: e.tensor_scalar(tiny[0:1, 8:9], tiny[0:1, 7:8], -1.0, None, ALU.mult),
              waits=(ev,), inc="nrm")
    ev = P.op("t", lambda e: e.matmul(pM[:, 0:1], onesrowS[:, :], tiny[0:1, 8:9], start=True, stop=True),
              waits=(ev, ev_cload, RES["pM"]), inc="cbm")
    ev_negC = P.op("v", lambda e: e.tensor_copy(negC[:, :], pM[:, 0:1]), waits=(ev,), inc="nrm")
    RES["pM"] = ev_negC

    NGC = cfg.GASH // cfg.GC
    ev_gamm = {}
    for ki in range(F):
        par = ki % 2
        ev_l = P.dma("sy", lambda e, ki=ki, par=par: e.dma_start(
            out=arf(WMG + par * cfg.GASH, cfg.GASH),
            in_=wmod_g[128 * ki:128 * (ki + 1), :]),
            waits=(ev_s1_last, ev_gamm.get(ki - 2)), inc=f"wgl{par}")
        assert NGC <= 2
        for cg in range(NGC):
            gbk = pE if cg == 0 else pF
            ev = P.op("t", lambda e, ki=ki, cg=cg, par=par, gbk=gbk: e.matmul(
                gbk[0:2, 0:cfg.GC], ctsf[:, 2 * ki:2 * ki + 2],
                arf(WMG + par * cfg.GASH, cfg.GASH)[:, cfg.GC * cg:cfg.GC * (cg + 1)],
                start=(ki == 0), stop=(ki == F - 1)),
                waits=(ev_l, ev_siluf), inc="gamm" if cg == NGC - 1 else None)
        ev_gamm[ki] = ev
    ev_gaev = None
    for cg in range(NGC):
        gbk = pE if cg == 0 else pF
        ev_gaev = P.op("v", lambda e, cg=cg, gbk=gbk: e.tensor_copy(
            modg[0:2, cfg.GC * cg:cfg.GC * (cg + 1)], gbk[0:2, 0:cfg.GC]),
            waits=(ev_gamm[F - 1],), inc="gaev")
    ev_gast = P.dma("sy", lambda e: e.dma_start(out=agg_i.ap(), in_=modg[0:2, :]),
                    waits=(ev_gaev,), inc="gast")
    ev_agg = P.op("g", lambda e: e.collective_compute(
        "AllGather", ALU.bypass, replica_groups=[list(range(NC))],
        ins=[agg_i.ap().opt()], outs=[agg_o.ap().opt()]),
        waits=(ev_gast,), inc="agg", amt=1, kind="d")
    ev_garb = P.dma("sy", lambda e: e.dma_start(out=modg[:, :], in_=agg_o.ap()),
                    waits=(ev_agg,), inc="garb")

    def columnize_gate(which, dst, scale):
        prev = None
        nb = D // cfg.BB
        for j in range(nb):
            col = cfg.BB * j
            r, loc = GRP * which + col // cfg.GASH, col % cfg.GASH
            evm = P.op("t", lambda e, r=r, loc=loc: e.matmul(
                pM[:, 0:cfg.BB], bselS[:, 128 * r:128 * (r + 1)],
                modg[:, loc:loc + cfg.BB], start=True, stop=True),
                waits=(ev_garb, RES["pM"], prev), inc="gcm")
            prev = P.op("v", lambda e, j=j: e.tensor_copy(
                arf(GAB, D)[:, cfg.BB * j:cfg.BB * (j + 1)], pM[:, 0:cfg.BB]),
                waits=(evm,), inc="gcv")
            RES["pM"] = prev
        evt = None
        for fi in range(F):
            evt = P.op("t", lambda e, fi=fi: e.transpose(
                pM[:, fi:fi + 1], arf(GAB, D)[0:1, 128 * fi:128 * (fi + 1)],
                identfS[0:1, 0:1]),
                waits=(prev, ev_cload, RES["pM"] if fi == 0 else None),
                inc="gct" if fi == F - 1 else None)
        ev = P.op("v", lambda e: e.tensor_copy(sm[:, 0:F], pM[:, 0:F]), waits=(evt,), inc="gcv")
        RES["pM"] = ev
        ev = P.op("v", lambda e: e.tensor_scalar(sm2[:, 0:F], sm[:, 0:F], 0.0, 2.0, ALU.is_ge, ALU.mult),
                  waits=(ev,), inc="gcv")
        ev = P.op("v", lambda e: e.tensor_scalar(sm2[:, 0:F], sm2[:, 0:F], -1.0, None, ALU.add),
                  waits=(ev,), inc="gcv")
        ev = P.op("v", lambda e: e.tensor_scalar(sm[:, F:2 * F], sm[:, 0:F], -1.0, None, ALU.mult),
                  waits=(ev,), inc="gcv")
        ev = P.op("v", lambda e: e.tensor_tensor(sm[:, 0:F], sm[:, 0:F], sm[:, F:2 * F], ALU.max),
                  waits=(ev,), inc="gcv")
        ev = P.op("v", lambda e: e.tensor_scalar(sm[:, 0:F], sm[:, 0:F], 0.25, float(scale), ALU.max, ALU.mult),
                  waits=(ev,), inc="gcv")
        ev = P.op("v", lambda e, dst=dst: e.tensor_tensor(dst[:, 0:F], sm[:, 0:F], sm2[:, 0:F], ALU.mult),
                  waits=(ev,), inc="gcv")
        return ev

    ev_ga = columnize_gate(0, gacol, 1.0)
    ev_gb = columnize_gate(1, gb4col, LORA_SCALE)

    ev_cbpb = bcast_gate(4, capb, True, extra_wait=ev_s1_last)
    ev_sbb = bcast_gate(3, sab, False, extra_wait=ev_s1_last)

    exp_by_bank = {}
    ev_prev_rcp = ev_gaev
    ev_prev_div = None
    prev_kv = {}
    prev_store = {}
    slot_busy = {}
    ev_aga = {}
    for qg in range(NQG):
        L = (qg + 1) * TOKG
        NKB = L // 128
        for h in range(HPC):
            it = qg * HPC + h
            par = it % 2
            kb_off = KBUF + par * S
            vb_off = VBUF + par * S
            qb_off = QBUF + par * 512
            gate = (ev_spw0, ev_spw1, prev_kv.get(par))
            ev_kl = P.dma("sy", lambda e, h=h, L=L, kb_off=kb_off: e.dma_start(
                out=arv(kb_off, L), in_=k_sp[h, :, 0:L]), waits=gate, inc=f"kl{par}")
            ev_vl = P.dma("sy", lambda e, h=h, L=L, vb_off=vb_off: e.dma_start(
                out=arv(vb_off, L), in_=v_sp[h, :, 0:L]), waits=gate, inc=f"vl{par}")
            ev_ql = P.dma("sy", lambda e, h=h, qg=qg, qb_off=qb_off: e.dma_start(
                out=arv(qb_off, 512), in_=q_sp[h, :, TOKG * qg:TOKG * (qg + 1)]),
                waits=gate, inc=f"ql{par}")
            ev_pv = None
            for kb in range(NKB):
                bpar = kb % 2
                bk = banksAB[bpar]
                slot = PRB + 512 * (kb % 3)
                ev_s = P.op("t", lambda e, kb=kb, bk=bk, kb_off=kb_off, qb_off=qb_off: e.matmul(
                    bk[:, :], arv(kb_off + 128 * kb, 128), arv(qb_off, 512),
                    start=True, stop=True),
                    waits=(ev_kl, ev_ql, exp_by_bank.get(bpar)), inc="scm")
                ev_e = P.op("sc", lambda e, bk=bk, slot=slot: e.activation(
                    arv(slot, 512), bk[:, :], AF.Exp, bias=negC[:, :],
                    scale=1.0 / math.sqrt(HD)),
                    waits=(ev_s, ev_negC, slot_busy.get(kb % 3)), inc="exp")
                exp_by_bank[bpar] = ev_e
                dj = kb - 4 * qg
                if dj >= 0:
                    ev_e = P.op("v", lambda e, slot=slot, dj=dj: e.tensor_tensor(
                        arv(slot, 512), arv(slot, 512),
                        masksS[:, TOKG * dj:TOKG * (dj + 1)], ALU.mult),
                        waits=(ev_e, ev_cload), inc="msk")
                P.op("t", lambda e, slot=slot, kb=kb, NKB=NKB: e.matmul(
                    pF[:, :], ones128S[:, :], arv(slot, 512),
                    start=(kb == 0), stop=(kb == NKB - 1)),
                    waits=(ev_e, ev_cload) + ((ev_prev_rcp,) if kb == 0 else ()),
                    inc="smm" if kb == NKB - 1 else None)
                ev_pv = P.op("t", lambda e, slot=slot, kb=kb, NKB=NKB, vb_off=vb_off: e.matmul(
                    pC[:, :], arv(vb_off + 128 * kb, 128), arv(slot, 512),
                    start=(kb == 0), stop=(kb == NKB - 1)),
                    waits=(ev_e, ev_vl) + ((ev_prev_div,) if kb == 0 else ()),
                    inc="pvm")
                slot_busy[kb % 3] = ev_pv
            ev_smm = P.ev("smm")
            prev_kv[par] = ev_pv
            ev_rc = P.op("v", lambda e: e.reciprocal(sm[:, :], pF[:, :]),
                         waits=(ev_smm,), inc="rcp")
            ev_prev_rcp = ev_rc
            ev_div = P.op("v", lambda e, par=par: e.tensor_tensor(
                arv(ATTS + 512 * par, 512), pC[:, :], sm[:, :], ALU.mult),
                waits=(ev_rc, ev_pv, prev_store.get(par)), inc="div")
            ev_prev_div = ev_div
            prev_store[par] = P.dma("sy", lambda e, qg=qg, h=h, par=par: e.dma_start(
                out=aga_i.ap()[qg, 128 * h:128 * (h + 1), :],
                in_=arv(ATTS + 512 * par, 512)),
                waits=(ev_div,), inc=f"aw{qg}")
        ev_aga[qg] = P.op("g", lambda e, qg=qg: e.collective_compute(
            "AllGather", ALU.bypass,
            replica_groups=[list(range(GRP)), list(range(GRP, 2 * GRP))],
            ins=[aga_i.ap()[qg].opt()], outs=[aga_o.ap()[qg].opt()]),
            waits=(P.ev(f"aw{qg}"),), inc="aga", amt=1, kind="d")
    ev_aga_all = P.ev("aga")

    aga_flat = aga_o.ap().rearrange("a b c -> (a b) c")
    ev_woc_prev = None
    ev_x2w = []
    ev_attg_free = None
    ev_ep_pt = st_rot.get("ptv")
    for og in range(NOG):
        ev_gath = None
        for fi in range(F):
            ev_gath = P.op("g", lambda e, og=og, fi=fi: e.indirect_dma_start(
                arv(ATTG + fi * 512, 512), None, aga_flat,
                IndirectOffsetOnAxis(ap=idxb[:, og * F + fi:og * F + fi + 1], axis=0)),
                waits=(ev_aga_all, ev_idx, ev_attg_free), inc="agard", amt=16, kind="d")
        for ocg in range(ND512):
            par = ocg % 2
            ev_wol = None
            for fi in range(F):
                ev_wol = P.dma("g", lambda e, ocg=ocg, fi=fi, par=par: e.dma_start(
                    out=arv(WOSL + par * F * 512 + fi * 512, 512),
                    in_=wo[128 * fi:128 * (fi + 1), 512 * ocg:512 * (ocg + 1)]),
                    waits=(ev_woc_prev,), inc=f"wol{par}")
            ev_fill = {}
            for oc in range(4):
                bk = banks6[oc]
                evf = None
                for fi in range(F):
                    evf = P.op("t", lambda e, fi=fi, oc=oc, par=par, bk=bk: e.matmul(
                        bk[:, :], arv(WOSL + par * F * 512 + fi * 512 + 128 * oc, 128),
                        arv(ATTG + fi * 512, 512),
                        start=(fi == 0), stop=(fi == F - 1)),
                        waits=(ev_wol, ev_gath, ev_x2w[-1] if ev_x2w else None),
                        inc="wocm" if fi == F - 1 else None)
                ev_fill[oc] = evf
            ev_woc_prev = ev_fill[3]
            ev_tt = {}
            for oc in range(4):
                gidx = ocg * 4 + oc
                e0 = P.op("v", lambda e, oc=oc, gidx=gidx: e.tensor_scalar(
                    arv(X2STG + 2048 + 512 * (oc % 2), 512), banks6[oc][:, :],
                    gacol[:, gidx:gidx + 1], None, ALU.mult),
                    waits=(ev_fill[oc], ev_ga, ev_tt.get(oc - 2)), inc="epv")
                evt = None
                for j in range(4):
                    evt = P.op("t", lambda e, j=j, oc=oc: e.transpose(
                        pT[:, 128 * j:128 * (j + 1)],
                        arv(X2STG + 2048 + 512 * (oc % 2), 512)[:, 128 * j:128 * (j + 1)],
                        identS[:, :]),
                        waits=(e0, ev_ep_pt), inc="eptr" if j == 3 else None)
                exl = P.dma("sy", lambda e, og=og, gidx=gidx, oc=oc: e.dma_start(
                    out=arf(XS + 512 * (oc % 2), 512).rearrange("p (j c) -> p j c", j=4),
                    in_=xown[og * 512:(og + 1) * 512, 128 * gidx:128 * (gidx + 1)]
                    .rearrange("(j p) c -> p j c", p=128)),
                    waits=(ev_tt.get(oc - 2),), inc=f"exl{oc % 2}")
                ev2 = P.op("v", lambda e, evt=evt, oc=oc: e.tensor_tensor(
                    arv(X2STG + 512 * oc, 512), pT[:, :],
                    arf(XS + 512 * (oc % 2), 512), ALU.add),
                    waits=(evt, exl, ev_x2w[-1] if ev_x2w else None), inc="x2v")
                ev_ep_pt = ev2
                ev_tt[oc] = ev2
            for j in range(4):
                ev_x2w.append(P.dma("sy", lambda e, ocg=ocg, og=og, j=j: e.dma_start(
                    out=x2_hb[og * 512 + 128 * j:og * 512 + 128 * (j + 1),
                              512 * ocg:512 * (ocg + 1)]
                    .rearrange("p (oc c) -> p oc c", oc=4),
                    in_=arv(X2STG, 2048).rearrange("p (oc j c) -> p j oc c", oc=4, j=4)[:, j]),
                    waits=(ev_tt[3],), inc="x2w"))
        ev_attg_free = ev_fill[3]
    ev_x2done = P.ev("x2w")

    ev_ffnw = None
    for fi in range(F):
        ev_ffnw = P.dma("g", lambda e, fi=fi: e.dma_start(
            out=arv(FFNW + fi * 64, 64), in_=Apk[128 * fi:128 * (fi + 1), :]),
            waits=(ev_woc_prev,), inc="ffnw")
        ev_ffnw = P.dma("g", lambda e, fi=fi: e.dma_start(
            out=arv(FFNW + F * 64 + fi * 4, 4), in_=routerT[128 * fi:128 * (fi + 1), :]),
            waits=(ev_woc_prev,), inc="ffnw")
    ev_ffnw = P.dma("g", lambda e: e.dma_start(out=BpkS[0:64, :], in_=Bpk),
                    waits=(ev_woc_prev,), inc="ffnw")
    ev_ffnw = P.ev("ffnw")

    ev_ffnd = {}
    ev_h2c_prev = ev_ep_pt
    ev_ln_prev = None
    ev_h2r_free = None
    ev_mid_prev = None
    ev_gex_prev = None
    ev_out_prev = None
    for tt in range(cfg.NOT):
        par = tt % 2
        ev_x2 = P.dma("sy", lambda e, tt=tt, par=par: e.dma_start(
            out=arv(X2T + par * D, D), in_=x2_hb[128 * tt:128 * (tt + 1), :]),
            waits=(ev_x2done, ev_ffnd.get(tt - 2)), inc=f"x2l{par}")
        ev_bs = None
        for i in range(ND512):
            ev_bs = P.op("v", lambda e, i=i, par=par: e.bn_stats(
                stats[:, 6 * i:6 * i + 6], arv(X2T + par * D, D)[:, 512 * i:512 * (i + 1)]),
                waits=(ev_x2, ev_ln_prev if i == 0 else None), inc="l2v")
        ev_ag = P.op("v", lambda e: e.bn_aggr(mv[:, 0:2], stats[:, :]), waits=(ev_bs,), inc="l2v")
        ev_ep = P.op("v", lambda e: e.tensor_scalar(mv[:, 2:3], mv[:, 1:2], LN_EPS, None, ALU.add),
                     waits=(ev_ag,), inc="l2v")
        ev_v = P.op("v", lambda e: e.tensor_scalar(mv[:, 3:4], mv[:, 0:1], -1.0, None, ALU.mult),
                    waits=(ev_ag,), inc="l2v")
        ev_sq = P.op("sc", lambda e: e.activation(mv[:, 4:5], mv[:, 2:3], AF.Sqrt), waits=(ev_v, ev_ep), inc="l2sq")
        ev_r = P.op("v", lambda e: e.reciprocal(mv[:, 5:6], mv[:, 4:5]), waits=(ev_sq,), inc="l2v")
        ev1 = P.op("v", lambda e, par=par: e.tensor_scalar(
            arv(H2R, D), arv(X2T + par * D, D), mv[:, 3:4], mv[:, 5:6], ALU.add, ALU.mult),
            waits=(ev_r, ev_h2r_free), inc="l2v")
        ev_ln_prev = ev1
        ev2 = P.op("v", lambda e: e.tensor_tensor(arv(H2R, D), arv(H2R, D), capb[:, :], ALU.mult),
                   waits=(ev1, ev_cbpb), inc="l2v")
        ev3 = P.op("v", lambda e: e.tensor_tensor(arv(H2R, D), arv(H2R, D), sab[:, :], ALU.add),
                   waits=(ev2, ev_sbb), inc="l2v")
        ev_h2 = None
        for g4 in range(F // 4):
            evt = None
            for j in range(4):
                fi = g4 * 4 + j
                evt = P.op("t", lambda e, fi=fi, j=j: e.transpose(
                    pT[:, 128 * j:128 * (j + 1)], arv(H2R, D)[:, 128 * fi:128 * (fi + 1)],
                    identS[:, :]),
                    waits=(ev3, ev_h2c_prev), inc="h2t" if j == 3 else None)
            ev_h2 = P.op("v", lambda e, g4=g4: e.tensor_copy(arv(H2C + 512 * g4, 512), pT[:, :]),
                         waits=(evt, ev_mid_prev), inc="h2c")
            ev_h2c_prev = ev_h2
            ev_h2r_free = evt

        def h2chunk(fi):
            return arv(H2C + 128 * fi, 128)

        evl = None
        for fi in range(F):
            evl = P.op("t", lambda e, fi=fi: e.matmul(
                pM[0:4, 0:128], arv(FFNW + F * 64 + fi * 4, 4), h2chunk(fi),
                start=(fi == 0), stop=(fi == F - 1)),
                waits=(ev_h2, ev_ffnw, RES["pM"] if fi == 0 else None),
                inc="lgm" if fi == F - 1 else None)
        ev_lg = P.op("v", lambda e: e.tensor_copy(gat[0:4, :], pM[0:4, 0:128]),
                     waits=(evl,), inc="lgv")
        ev_lt = P.op("t", lambda e: e.transpose(pM[:, 8:12], gat[0:4, :], identfS[0:4, 0:4]),
                     waits=(ev_lg,), inc="lgt")
        ev = P.op("v", lambda e: e.tensor_copy(lgt[:, 0:4], pM[:, 8:12]), waits=(ev_lt,), inc="gv")
        RES["pM"] = ev
        ev = P.op("v", lambda e: e.tensor_tensor(lgt[:, 4:5], lgt[:, 0:1], lgt[:, 1:2], ALU.max), waits=(ev,), inc="gv")
        ev = P.op("v", lambda e: e.tensor_tensor(lgt[:, 6:7], lgt[:, 2:3], lgt[:, 3:4], ALU.max), waits=(ev,), inc="gv")
        ev = P.op("v", lambda e: e.tensor_tensor(lgt[:, 5:6], lgt[:, 0:1], lgt[:, 1:2], ALU.min), waits=(ev,), inc="gv")
        ev = P.op("v", lambda e: e.tensor_tensor(lgt[:, 7:8], lgt[:, 2:3], lgt[:, 3:4], ALU.min), waits=(ev,), inc="gv")
        ev = P.op("v", lambda e: e.tensor_tensor(lgt[:, 8:9], lgt[:, 4:5], lgt[:, 6:7], ALU.max), waits=(ev,), inc="gv")
        ev = P.op("v", lambda e: e.tensor_tensor(lgt[:, 9:10], lgt[:, 4:5], lgt[:, 6:7], ALU.min), waits=(ev,), inc="gv")
        ev = P.op("v", lambda e: e.tensor_tensor(lgt[:, 10:11], lgt[:, 5:6], lgt[:, 7:8], ALU.max), waits=(ev,), inc="gv")
        ev = P.op("v", lambda e: e.tensor_tensor(lgt[:, 11:12], lgt[:, 9:10], lgt[:, 10:11], ALU.max), waits=(ev,), inc="gv")
        ev = P.op("v", lambda e: e.tensor_scalar(mv[:, 6:7], lgt[:, 8:9], -1.0, None, ALU.mult), waits=(ev,), inc="gv")
        ev_e = P.op("sc", lambda e: e.activation(lgt[:, 12:16], lgt[:, 0:4], AF.Exp, bias=mv[:, 6:7]),
                    waits=(ev,), inc="gse")
        ev = P.op("v", lambda e: e.tensor_scalar(lgt[:, 0:4], lgt[:, 0:4], lgt[:, 11:12], None, ALU.is_ge),
                  waits=(ev_e,), inc="gv")
        ev = P.op("v", lambda e: e.tensor_tensor(lgt[:, 12:16], lgt[:, 12:16], lgt[:, 0:4], ALU.mult),
                  waits=(ev,), inc="gv")
        ev = P.op("v", lambda e: e.reduce_sum(mv[:, 7:8], lgt[:, 12:16], axis=AX.X), waits=(ev,), inc="gv")
        ev = P.op("v", lambda e: e.reciprocal(mv[:, 6:7], mv[:, 7:8]), waits=(ev,), inc="gv")
        ev = P.op("v", lambda e: e.tensor_scalar(lgt[:, 12:16], lgt[:, 12:16], mv[:, 6:7], None, ALU.mult),
                  waits=(ev,), inc="gv")
        ev_gt = P.op("t", lambda e: e.transpose(pM[0:4, 16:144], lgt[:, 12:16], identfS[:, :]),
                     waits=(ev, RES["pM"]), inc="gtt")
        ev_gt = P.op("v", lambda e: e.tensor_copy(gat[0:4, :], pM[0:4, 16:144]), waits=(ev_gt,), inc="gv")
        RES["pM"] = ev_gt
        ev_ge = P.op("t", lambda e: e.matmul(pE[0:64, 0:128], expandS[:, :], gat[0:4, :],
                                             start=True, stop=True),
                     waits=(ev_gt, ev_cload, ev_gex_prev, ev_gaev), inc="gexm")
        ev_ge = P.op("v", lambda e: e.tensor_copy(sm[0:64, 0:128], pE[0:64, 0:128]),
                     waits=(ev_ge,), inc="gexv")
        ev_gex_prev = ev_ge
        evm = None
        for fi in range(F):
            evm = P.op("t", lambda e, fi=fi: e.matmul(
                pD[0:64, 0:128], arv(FFNW + fi * 64, 64), h2chunk(fi),
                start=(fi == 0), stop=(fi == F - 1)),
                waits=(ev_h2, ev_ffnw, ev_mid_prev), inc="midm" if fi == F - 1 else None)
        ev_mid = P.op("v", lambda e: e.tensor_tensor(
            arv(MIDB, 128)[0:64, :], pD[0:64, 0:128], sm[0:64, 0:128], ALU.mult),
            waits=(evm, ev_ge), inc="midv")
        ev_mid_prev = ev_mid
        ev_dtr_prev = None
        for oc in range(F):
            ev_d = P.op("t", lambda e, oc=oc: e.matmul(
                pC[:, 0:128], BpkS[0:64, 128 * oc:128 * (oc + 1)], arv(MIDB, 128)[0:64, :],
                start=True, stop=True),
                waits=(ev_mid, ev_ffnw, ev_dtr_prev), inc="dm")
            ev_d2 = P.op("v", lambda e, oc=oc: e.tensor_scalar(
                arv(MIDB + 128, 128), pC[:, 0:128], gb4col[:, oc:oc + 1], None, ALU.mult),
                waits=(ev_d, ev_gb, ev_dtr_prev), inc="dts")
            ev_d3 = P.op("t", lambda e: e.transpose(pT[:, 0:128], arv(MIDB + 128, 128), identS[:, :]),
                         waits=(ev_d2, ev_out_prev), inc="dtr")
            ev_dtr_prev = ev_d3
            ev_d4 = P.op("v", lambda e, oc=oc, par=par: e.tensor_tensor(
                arf(OUTS, D)[:, 128 * oc:128 * (oc + 1)], pT[:, 0:128],
                arv(X2T + par * D, D)[:, 128 * oc:128 * (oc + 1)], ALU.add),
                waits=(ev_d3, ev_x2) + ((P.ev("outw"),) if oc == 0 else ()),
                inc="outv")
            ev_out_prev = ev_d4
        ev_ffnd[tt] = ev_out_prev
        P.dma("sy", lambda e, tt=tt: e.dma_start(
            out=OUT[128 * tt:128 * (tt + 1), :], in_=arf(OUTS, D)),
            waits=(ev_out_prev,), inc="outw")

    P.op("sy", None, waits=(P.ev("outw"),))
    P.emit(nc, ctx)
    return nc, ctx


_BUILD_CACHE = {}


def get_program(key):
    if key not in _BUILD_CACHE:
        cfg = CFG(*key)
        nc, _ = build(cfg)
        _BUILD_CACHE[key] = (nc, cfg)
    return _BUILD_CACHE[key]


def kernel(L0_attn_hidden_states, L0_attn_wq, L0_attn_wk, L0_attn_wv,
           L0_attn_wo, L0_attn_kq1, L0_attn_kq2, L0_attn_kk1, L0_attn_kk2,
           L0_attn_scalingD, L0_attn_interaction_k, L0_attn_scalingD2,
           L0_ffn_router_w, L0_ffn_A, L0_ffn_B, L0_c, L0_wmod):
    x = np.asarray(L0_attn_hidden_states, dtype=np.float32)
    B, S, D = x.shape
    nc, cfg = get_program((D, S))
    T = dict(x=x, wq=np.asarray(L0_attn_wq, np.float32),
             wk=np.asarray(L0_attn_wk, np.float32),
             wv=np.asarray(L0_attn_wv, np.float32),
             wo=np.asarray(L0_attn_wo, np.float32),
             wmod=np.asarray(L0_wmod, np.float32),
             c=np.asarray(L0_c, np.float32),
             router_w=np.asarray(L0_ffn_router_w, np.float32),
             A=np.asarray(L0_ffn_A, np.float32),
             B=np.asarray(L0_ffn_B, np.float32))
    consts = host_constants(cfg)
    in_maps = [host_core_inputs(cfg, c_, T, consts) for c_ in range(NC)]
    res = run_bass_kernel_spmd(nc, in_maps, core_ids=list(range(NC)))
    out = np.empty((B, S, D), dtype=np.float32)
    for c_ in range(NC):
        b, p = c_ // GRP, c_ % GRP
        out[b, cfg.OWN * p: cfg.OWN * (p + 1)] = res.results[c_]["out"]
    return out


# version 04
# speedup vs baseline: 1.3241x; 1.3241x over previous
import math
import contextlib
import numpy as np
import ml_dtypes
import concourse.bass as bass
import concourse.mybir as mybir
from concourse.bass_utils import run_bass_kernel_spmd
from concourse.bass import IndirectOffsetOnAxis

F32 = mybir.dt.float32
BF16 = mybir.dt.bfloat16
U32 = mybir.dt.uint32
AF = mybir.ActivationFunctionType
ALU = mybir.AluOpType
AX = mybir.AxisListType
BF = ml_dtypes.bfloat16

NC = 8
GRP = 4
HD = 128
TOKG = 512
LN_EPS = 1e-5
LORA_SCALE = 4.0


class CFG:
    def __init__(self, D=4096, S=4096):
        self.D, self.S = D, S
        self.F = D // 128
        self.H = D // HD
        self.HPC = self.H // GRP
        self.WC = self.HPC * HD
        self.NTG = S // TOKG
        self.NQG = S // TOKG
        self.OWN = S // GRP
        self.NOT = self.OWN // 128
        self.NOG = self.OWN // TOKG
        self.MODSH = 6 * D // NC
        self.GASH = D // GRP
        self.BB = 512 if self.MODSH % 512 == 0 else 128
        self.GC = min(512, self.GASH)
        assert S % TOKG == 0 and D % 512 == 0 and self.OWN % TOKG == 0
        assert self.MODSH % self.BB == 0 and self.GASH % self.GC == 0
        assert self.F % 4 == 0 and self.HPC >= 1


class Prog:
    ENGINES = ("t", "v", "sc", "sy", "g")

    CSEM = {"t": "tC", "v": "vC", "sc": "scC", "sy": "syC", "g": "gC"}

    def __init__(self):
        self.ops = {e: [] for e in self.ENGINES}
        self.cnt = {}
        self.kind = {}
        self.owner = {}
        self.last_ev = {}

    def op(self, eng, fn, waits=(), inc=None, amt=1, kind="c"):
        ev = None
        if inc is not None:
            sem = self.CSEM[eng] if kind == "c" else inc
            if sem in self.owner:
                assert self.owner[sem] == eng and self.kind[sem] == kind, sem
            self.owner[sem] = eng
            self.kind[sem] = kind
            self.cnt[sem] = self.cnt.get(sem, 0) + amt
            ev = (sem, self.cnt[sem])
            self.last_ev[inc] = ev
        ws = tuple(w for w in waits if w is not None and w[1] > 0)
        self.ops[eng].append((ws, fn, ev[0] if ev else None, amt))
        return ev

    def dma(self, eng, fn, waits=(), inc=None):
        return self.op(eng, fn, waits, inc, 16, "d")

    def ev(self, name):
        if name in self.cnt:
            return (name, self.cnt[name])
        return self.last_ev.get(name)

    def emit(self, nc, ctx):
        sems = {}
        for name in self.cnt:
            sems[name] = ctx.enter_context(nc.semaphore(name))
        block = ctx.enter_context(nc.Block())

        def section(eng_name):
            def body(eng):
                seen = {}
                for ws, fn, inc, amt in self.ops[eng_name]:
                    for sem, val in ws:
                        if seen.get(sem, -1) >= val:
                            continue
                        eng.wait_ge(sems[sem], val)
                        seen[sem] = val
                    if fn is None:
                        continue
                    ins = fn(eng)
                    if inc is not None:
                        ins.then_inc(sems[inc], amt)
            return body

        block.tensor(section("t"))
        block.vector(section("v"))
        block.scalar(section("sc"))
        block.sync(section("sy"))
        block.gpsimd(section("g"))


def host_constants(cfg):
    D, S = cfg.D, cfg.S
    c = {}
    c["ident"] = np.eye(128, dtype=BF)
    c["identf"] = np.eye(128, dtype=np.float32)
    c["ones128"] = np.ones((128, 128), dtype=BF)
    c["onesrow"] = np.ones((1, 128), dtype=np.float32)
    P = np.zeros((128, 128), dtype=np.float32)
    for p in range(64):
        P[p + 64, p] = -1.0
        P[p, p + 64] = 1.0
    c["rotperm"] = P.astype(BF)
    inv = 1.0 / 10000.0 ** (np.arange(0, HD, 2, dtype=np.float64) / HD)
    invf = np.concatenate([inv, inv])
    ang = invf[:, None] * np.arange(S, dtype=np.float64)[None, :]
    c["cosT"] = np.cos(ang).astype(BF)
    c["sinT"] = np.sin(ang).astype(BF)
    nj = TOKG // 128
    m = np.zeros((128, nj * TOKG), dtype=BF)
    for j in range(nj):
        kp = np.arange(128)[:, None] + 128 * j
        qf = np.arange(TOKG)[None, :]
        m[:, j * TOKG:(j + 1) * TOKG] = (kp <= qf).astype(BF)
    c["masks"] = m
    c["expand"] = np.repeat(np.eye(4, dtype=np.float32), 16, axis=1)
    c["iota"] = np.arange(128, dtype=np.float32).reshape(128, 1)
    return c


def host_core_inputs(cfg, core, T, consts):
    D = cfg.D
    b, p = core // GRP, core % GRP
    x = np.ascontiguousarray(T["x"][b])
    wsl = slice(cfg.WC * p, cfg.WC * (p + 1))
    d = dict(consts)
    d["xb"] = x
    d["xown"] = np.ascontiguousarray(x[cfg.OWN * p: cfg.OWN * (p + 1)])
    d["wq"] = np.ascontiguousarray(T["wq"][:, wsl])
    d["wk"] = np.ascontiguousarray(T["wk"][:, wsl])
    d["wv"] = np.ascontiguousarray(T["wv"][:, wsl])
    d["wo"] = T["wo"]
    d["wmod_s"] = np.ascontiguousarray(
        T["wmod"][:, cfg.MODSH * core: cfg.MODSH * (core + 1)])
    gbase = (2 * D if core < GRP else 5 * D) + cfg.GASH * (core % GRP)
    d["wmod_g"] = np.ascontiguousarray(T["wmod"][:, gbase: gbase + cfg.GASH])
    d["cT"] = np.ascontiguousarray(T["c"].T)
    d["routerT"] = np.ascontiguousarray(T["router_w"].T)
    d["Apk"] = np.ascontiguousarray(T["A"].transpose(2, 0, 1).reshape(D, 64))
    d["Bpk"] = np.ascontiguousarray(T["B"].transpose(0, 2, 1).reshape(64, D))
    bs = np.zeros((2 * NC, NC * 128), dtype=np.float32)
    for r in range(NC):
        bs[2 * r + b, 128 * r: 128 * (r + 1)] = 1.0
    d["bsel16"] = bs
    d["baseq"] = np.stack(
        [np.full(128, float((cfg.NOG * p + k) * D), dtype=np.float32)
         for k in range(cfg.NOG)], axis=1)
    return d


def build(cfg):
    D, S, F, HPC, WC = cfg.D, cfg.S, cfg.F, cfg.HPC, cfg.WC
    NTG, NQG, OWN, NOG = cfg.NTG, cfg.NQG, cfg.OWN, cfg.NOG
    NJ = TOKG // 128
    ND512 = D // 512
    nc = bass.Bass("TRN2", target_bir_lowering=False, debug=False)
    ctx = contextlib.ExitStack()

    def din(name, shape, dt=F32):
        return nc.dram_tensor(name, shape, dt, kind="ExternalInput").ap()

    xb = din("xb", [S, D]); xown = din("xown", [OWN, D])
    wq = din("wq", [D, WC]); wk = din("wk", [D, WC]); wv = din("wv", [D, WC])
    wo = din("wo", [D, D])
    wmod_s = din("wmod_s", [D, cfg.MODSH]); wmod_g = din("wmod_g", [D, cfg.GASH])
    cT = din("cT", [D, 2])
    routerT = din("routerT", [D, 4]); Apk = din("Apk", [D, 64]); Bpk = din("Bpk", [64, D])
    ident = din("ident", [128, 128], BF16); identf = din("identf", [128, 128], F32)
    ones128 = din("ones128", [128, 128], BF16); onesrow = din("onesrow", [1, 128], F32)
    rotperm = din("rotperm", [128, 128], BF16)
    cosT = din("cosT", [128, S], BF16); sinT = din("sinT", [128, S], BF16)
    masks = din("masks", [128, NJ * TOKG], BF16)
    expand = din("expand", [4, 64], F32); iota = din("iota", [128, 1], F32)
    bsel16 = din("bsel16", [2 * NC, NC * 128], F32)
    baseq = din("baseq", [128, NOG], F32)
    OUT = nc.dram_tensor("out", [OWN, D], F32, kind="ExternalOutput").ap()

    h_fm = nc.dram_tensor("h_fm", [F, 128, S], BF16).ap()
    wo_bf = nc.dram_tensor("wo_bf", [D, D], BF16).ap()
    q_sp = nc.dram_tensor("q_sp", [HPC, 128, S], BF16).ap()
    k_sp = nc.dram_tensor("k_sp", [HPC, 128, S], BF16).ap()
    v_sp = nc.dram_tensor("v_sp", [HPC, 128, S], BF16).ap()
    x2_hb = nc.dram_tensor("x2_hb", [OWN, D], BF16).ap()
    agm_i = nc.dram_tensor("agm_i", [2, cfg.MODSH], F32)
    agm_o = nc.dram_tensor("agm_o", [2 * NC, cfg.MODSH], F32)
    agg_i = nc.dram_tensor("agg_i", [2, cfg.GASH], F32)
    agg_o = nc.dram_tensor("agg_o", [2 * NC, cfg.GASH], F32)
    aga_i = nc.dram_tensor("aga_i", [NQG, WC, TOKG], BF16)
    aga_o = nc.dram_tensor("aga_o", [NQG, GRP * WC, TOKG], BF16)

    pA = ctx.enter_context(nc.psum_tensor("pA", [128, 512], F32))
    pB = ctx.enter_context(nc.psum_tensor("pB", [128, 512], F32))
    pC = ctx.enter_context(nc.psum_tensor("pC", [128, 512], F32))
    pD = ctx.enter_context(nc.psum_tensor("pD", [128, 512], F32))
    pE = ctx.enter_context(nc.psum_tensor("pE", [128, 512], F32))
    pF = ctx.enter_context(nc.psum_tensor("pF", [128, 512], F32))
    pM = ctx.enter_context(nc.psum_tensor("pM", [128, 512], F32))
    pT = ctx.enter_context(nc.psum_tensor("pT", [128, 512], BF16))
    banksAB = [pA, pB]
    banks6 = [pA, pB, pC, pD, pE, pF]

    sb = lambda n, s, d: ctx.enter_context(nc.sbuf_tensor(n, s, d))
    identS = sb("identS", [128, 128], BF16)
    identfS = sb("identfS", [128, 128], F32)
    ones128S = sb("ones128S", [128, 128], BF16)
    onesrowS = sb("onesrowS", [1, 128], F32)
    rotpermS = sb("rotpermS", [128, 128], BF16)
    masksS = sb("masksS", [128, NJ * TOKG], BF16)
    expandS = sb("expandS", [4, 64], F32)
    iotaS = sb("iotaS", [128, 1], F32)
    bselS = sb("bselS", [2 * NC, NC * 128], F32)
    baseqS = sb("baseqS", [128, NOG], F32)
    cosb = sb("cosb", [128, 512], BF16)
    sinb = sb("sinb", [128, 512], BF16)
    capb = sb("capb", [128, D], BF16)
    sab = sb("sab", [128, D], BF16)
    gacol = sb("gacol", [128, max(cfg.F, 2)], F32)
    gb4col = sb("gb4col", [128, max(cfg.F, 2)], F32)
    negC = sb("negC", [128, 1], F32)
    idxb = sb("idxb", [128, max(NOG * F, 4)], U32)
    cts = sb("cts", [128, 2 * F], F32)
    ctsb = sb("ctsb", [128, 2 * F], BF16)
    ctsf = sb("ctsf", [128, 2 * F], F32)
    stats = sb("stats", [128, 6 * ND512], F32)
    mv = sb("mv", [128, 8], F32)
    tiny = sb("tiny", [128, 16], F32)
    sm = sb("sm", [128, 512], F32)
    sm2 = sb("sm2", [128, max(cfg.F, 2)], F32)
    modsh = sb("modsh", [2 * NC, cfg.MODSH], F32)
    modg = sb("modg", [2 * NC, cfg.GASH], F32)
    lgt = sb("lgt", [128, 16], F32)
    gat = sb("gat", [4, 128], F32)

    ARN = 64512
    AR = sb("AR", [128, ARN], BF16)
    ARFN = 8192
    ARF = sb("ARF", [128, ARFN], F32)

    def arv(off, n):
        assert off + n <= ARN, (off, n)
        return AR[:, off:off + n]

    def arf(off, n):
        assert off + n <= ARFN, (off, n)
        return ARF[:, off:off + n]

    WSLAB = 0
    HTG = F * WC
    S2STG = HTG + F * 512
    HTMS1 = S2STG + 4096
    S1STG = HTMS1 + D
    WMODBUF = S1STG + 1024
    assert WMODBUF + 2 * cfg.MODSH <= ARN
    KBUF = 0
    VBUF = 2 * S
    QBUF = 4 * S
    PRB = QBUF + 1024
    ATTS = PRB + 2048
    ATTG = 0
    WOSL = ATTS + 1024
    X2STG = WOSL + 2 * F * 512
    assert X2STG + 3584 <= ARN
    H2C = F * 512
    FFNW = H2C + F * 128
    BPK = FFNW + F * 68
    MIDB = BPK + D
    X2T = MIDB + 512
    H2R = X2T + 2 * D
    assert H2R + D <= ARN
    BpkS = AR[:, BPK:BPK + D]

    XT = 0
    GAB = 0
    WMG = D
    XS = D + 2048
    OUTS = D
    assert max(2 * D, D + 2048 + 1024, GAB + D, OUTS + D) <= ARFN

    P = Prog()
    E = {}
    RES = {"pM": None}

    for dst, src in ((identS, ident), (identfS, identf), (ones128S, ones128),
                     (onesrowS, onesrow), (rotpermS, rotperm),
                     (masksS, masks), (expandS, expand),
                     (iotaS, iota), (bselS, bsel16), (baseqS, baseq)):
        P.dma("sy", lambda e, d=dst, s=src: e.dma_start(out=d[:, :], in_=s), inc="cload")
    ev_cload = P.ev("cload")

    for k in range(NOG):
        for fi in range(F):
            P.op("v", lambda e, k=k, fi=fi: e.tensor_scalar(
                idxb[:, k * F + fi: k * F + fi + 1], iotaS[:, :],
                baseqS[:, k:k + 1], float(fi * 128), ALU.add, ALU.add),
                waits=(ev_cload,), inc="idx")
    ev_idx = P.ev("idx")

    P.dma("sy", lambda e: e.dma_start(
        out=cts[:, :].rearrange("p (k b) -> p k b", k=F),
        in_=cT.rearrange("(k p) b -> p k b", p=128)), inc="ctl")
    ev_silu = P.op("sc", lambda e: e.activation(ctsb[:, :], cts[:, :], AF.Silu),
                   waits=(P.ev("ctl"),), inc="silu")
    ev_siluf = P.op("sc", lambda e: e.activation(ctsf[:, :], cts[:, :], AF.Silu),
                    waits=(P.ev("ctl"),), inc="siluf")

    NCG = cfg.MODSH // cfg.BB
    ev_modmm = {}
    for ki in range(F):
        par = ki % 2
        off = WMODBUF + par * cfg.MODSH
        ev_l = P.dma("g", lambda e, ki=ki, off=off: e.dma_start(
            out=arv(off, cfg.MODSH), in_=wmod_s[128 * ki:128 * (ki + 1), :]),
            waits=(ev_modmm.get(ki - 2),), inc=f"wml{par}")
        for cg in range(NCG):
            bk = banks6[cg % 6]
            ev = P.op("t", lambda e, ki=ki, cg=cg, off=off, bk=bk: e.matmul(
                bk[0:2, 0:cfg.BB], ctsb[:, 2 * ki:2 * ki + 2],
                arv(off, cfg.MODSH)[:, cfg.BB * cg:cfg.BB * (cg + 1)],
                start=(ki == 0), stop=(ki == F - 1)),
                waits=(ev_silu, ev_l), inc="modmm" if cg == NCG - 1 else None)
        ev_modmm[ki] = ev
    ev_modev = None
    for cg in range(NCG):
        bk = banks6[cg % 6]
        ev_modev = P.op("v", lambda e, cg=cg, bk=bk: e.tensor_copy(
            modsh[0:2, cfg.BB * cg:cfg.BB * (cg + 1)], bk[0:2, 0:cfg.BB]),
            waits=(ev_modmm[F - 1],), inc="modev")
    ev_modst = P.dma("sy", lambda e: e.dma_start(out=agm_i.ap(), in_=modsh[0:2, :]),
                     waits=(ev_modev,), inc="modst")
    ev_agm = P.op("g", lambda e: e.collective_compute(
        "AllGather", ALU.bypass, replica_groups=[list(range(NC))],
        ins=[agm_i.ap().opt()], outs=[agm_o.ap().opt()]),
        waits=(ev_modst,), inc="agm", amt=1, kind="d")
    ev_modrb = P.dma("sy", lambda e: e.dma_start(out=modsh[:, :], in_=agm_o.ap()),
                     waits=(ev_agm,), inc="modrb")

    def bcast_gate(gate_idx, dst, add1, extra_wait=None):
        prev = None
        nb = D // cfg.BB
        for j in range(nb):
            col = gate_idx * D + cfg.BB * j
            r, loc = col // cfg.MODSH, col % cfg.MODSH
            evm = P.op("t", lambda e, r=r, loc=loc: e.matmul(
                pM[:, 0:cfg.BB], bselS[:, 128 * r:128 * (r + 1)],
                modsh[:, loc:loc + cfg.BB], start=True, stop=True),
                waits=(ev_modrb, ev_cload, RES["pM"], prev, extra_wait), inc="bgm")
            if add1:
                prev = P.op("v", lambda e, j=j, dst=dst: e.tensor_scalar(
                    dst[:, cfg.BB * j:cfg.BB * (j + 1)], pM[:, 0:cfg.BB],
                    1.0, None, ALU.add), waits=(evm,), inc="bgv")
            else:
                prev = P.op("v", lambda e, j=j, dst=dst: e.tensor_copy(
                    dst[:, cfg.BB * j:cfg.BB * (j + 1)], pM[:, 0:cfg.BB]),
                    waits=(evm,), inc="bgv")
            RES["pM"] = prev
        return prev

    ev_capb = bcast_gate(1, capb, True)
    ev_sab = bcast_gate(0, sab, False)

    NXT = S // 128
    ev_s1done = {}
    ev_ln_prev = None
    ev_ht_free = None
    ev_hw = {}
    ev_pt_free = None
    ev_hst = [None, None]
    ev_s1_last = None
    for xt in range(NXT):
        par = xt % 2
        tg = (xt * 128) // TOKG
        ev_xl = P.dma("sy", lambda e, xt=xt, par=par: e.dma_start(
            out=arf(XT + par * D, D), in_=xb[128 * xt:128 * (xt + 1), :]),
            waits=(ev_s1done.get(xt - 2),), inc=f"xld{par}")
        ev_bs = None
        for i in range(ND512):
            ev_bs = P.op("v", lambda e, i=i, par=par: e.bn_stats(
                stats[:, 6 * i:6 * i + 6],
                arf(XT + par * D, D)[:, 512 * i:512 * (i + 1)]),
                waits=(ev_xl, ev_ln_prev if i == 0 else None), inc="s1v")
        ev_ag = P.op("v", lambda e: e.bn_aggr(mv[:, 0:2], stats[:, :]),
                     waits=(ev_bs,), inc="s1v")
        ev_ep = P.op("v", lambda e: e.tensor_scalar(mv[:, 2:3], mv[:, 1:2], LN_EPS, None, ALU.add),
                     waits=(ev_ag,), inc="s1v")
        ev_v = P.op("v", lambda e: e.tensor_scalar(mv[:, 3:4], mv[:, 0:1], -1.0, None, ALU.mult),
                    waits=(ev_ag,), inc="s1v")
        ev_sq = P.op("sc", lambda e: e.activation(mv[:, 4:5], mv[:, 2:3], AF.Sqrt),
                     waits=(ev_v, ev_ep), inc="s1sq")
        ev_r = P.op("v", lambda e: e.reciprocal(mv[:, 5:6], mv[:, 4:5]), waits=(ev_sq,), inc="s1v")
        ev1 = P.op("v", lambda e, par=par: e.tensor_scalar(
            arv(HTMS1, D), arf(XT + par * D, D), mv[:, 3:4], mv[:, 5:6],
            ALU.add, ALU.mult), waits=(ev_r, ev_ht_free), inc="s1v")
        ev_ln_prev = ev1
        ev2 = P.op("v", lambda e: e.tensor_tensor(arv(HTMS1, D), arv(HTMS1, D), capb[:, :], ALU.mult),
                   waits=(ev1, ev_capb), inc="s1v")
        ev3 = P.op("v", lambda e: e.tensor_tensor(arv(HTMS1, D), arv(HTMS1, D), sab[:, :], ALU.add),
                   waits=(ev2, ev_sab), inc="s1v")
        ev_s1done[xt] = ev3
        ev_s1_last = ev3
        for g4 in range(F // 4):
            evt = None
            for j in range(4):
                fi = g4 * 4 + j
                evt = P.op("t", lambda e, fi=fi, j=j: e.transpose(
                    pT[:, 128 * j:128 * (j + 1)],
                    arv(HTMS1, D)[:, 128 * fi:128 * (fi + 1)], identS[:, :]),
                    waits=(ev3, ev_cload, ev_pt_free), inc="s1tr" if j == 3 else None)
            ev_ht_free = evt
            sl = g4 % 2
            evc = P.op("v", lambda e, sl=sl: e.tensor_copy(arv(S1STG + sl * 512, 512), pT[:, :]),
                       waits=(evt, ev_hst[sl]), inc="s1hc")
            ev_pt_free = evc
            ev_hst[sl] = P.dma("sy", lambda e, xt=xt, g4=g4, sl=sl: e.dma_start(
                out=h_fm[4 * g4:4 * g4 + 4, :, 128 * xt:128 * (xt + 1)].rearrange("f p t -> p f t"),
                in_=arv(S1STG + sl * 512, 512).rearrange("p (f t) -> p f t", f=4)),
                waits=(evc,), inc=f"hw{tg}")
    for tg in range(NTG):
        ev_hw[tg] = P.ev(f"hw{tg}")

    HF = F // 2
    ev_modev_all = P.ev("modev")

    def wslab_load(Wap, half, tag, wait_evs):
        ev = None
        for fi in range(HF):
            f = half * HF + fi
            ev = P.dma("g", lambda e, f=f, fi=fi, half=half, Wap=Wap: e.dma_start(
                out=arv(WSLAB + half * HF * WC + fi * WC, WC),
                in_=Wap[128 * f:128 * (f + 1), :]),
                waits=wait_evs, inc=f"wsl{tag}{half}")
        return ev

    P.op("v", lambda e: e.memset(tiny[0:1, 0:2], 0.0), inc="nrm")
    st_rot = {}
    ev_nrm_last = None
    ev_spill = {"q": None, "k": None, "v": None}
    ev_Wmm = {}

    def do_pass(W, spill, rot, first_mm_extra=()):
        nonlocal ev_nrm_last
        ni = 0 if W == "q" else 1
        ev_htg_prev_consumed = None
        for tg in range(NTG):
            tok0 = TOKG * tg
            ev_h = None
            for fi in range(F):
                ev_h = P.dma("sy", lambda e, fi=fi, tg=tg: e.dma_start(
                    out=arv(HTG + fi * 512, 512),
                    in_=h_fm[fi, :, tok0 + 0:tok0 + 512] if False else
                        h_fm[fi, :, TOKG * tg:TOKG * (tg + 1)]),
                    waits=(ev_hw[tg], ev_htg_prev_consumed), inc="htg")
            if rot:
                P.dma("sy", lambda e, tok0=tok0: e.dma_start(
                    out=cosb[:, :], in_=cosT[:, tok0:tok0 + 512]),
                    waits=(st_rot.get("cos_free"),), inc="cosl")
                P.dma("sy", lambda e, tok0=tok0: e.dma_start(
                    out=sinb[:, :], in_=sinT[:, tok0:tok0 + 512]),
                    waits=(st_rot.get("cos_free"),), inc="cosl")
                ev_cos = P.ev("cosl")
            ev_last_mm = None
            for cc in range(HPC):
                it = tg * HPC + cc
                par = it % 2
                bk = banksAB[par]
                SS = S2STG + 2048 * par
                extra = first_mm_extra if (tg * HPC + cc) < 2 else ()
                for f in range(F):
                    half = f // HF
                    ev_last_mm = P.op("t", lambda e, f=f, cc=cc, bk=bk, half=half: e.matmul(
                        bk[:, :], arv(WSLAB + half * HF * WC + (f % HF) * WC + cc * 128, 128),
                        arv(HTG + f * 512, 512),
                        start=(f == 0), stop=(f == F - 1)),
                        waits=(ev_h, E[f"wsl{W}{half}"],
                               st_rot.get(("evac", par))) + tuple(extra),
                        inc=f"{W}mm" if f == F - 1 else None)
                evm = ev_last_mm
                if rot:
                    e0 = P.op("v", lambda e, bk=bk, SS=SS: e.tensor_copy(arv(SS, 512), bk[:, :]),
                              waits=(evm, st_rot.get(("rd", par))), inc="s2v")
                    e1 = P.op("v", lambda e, SS=SS: e.tensor_tensor(
                        arv(SS + 512, 512), arv(SS, 512), arv(SS, 512), ALU.mult),
                        waits=(e0,), inc="s2v")
                    e2 = P.op("t", lambda e, SS=SS: e.matmul(
                        pM[0:1, :], ones128S[:, 0:1], arv(SS + 512, 512),
                        start=True, stop=True),
                        waits=(e1, RES["pM"]), inc="s2nt")
                    e3 = P.op("v", lambda e: e.reduce_max(tiny[0:1, 4:5], pM[0:1, :], axis=AX.X),
                              waits=(e2, ev_nrm_last), inc="s2v")
                    e4 = P.op("v", lambda e, ni=ni: e.tensor_tensor(
                        tiny[0:1, ni:ni + 1], tiny[0:1, ni:ni + 1], tiny[0:1, 4:5], ALU.max),
                        waits=(e3,), inc="nrm")
                    ev_nrm_last = e4
                    e5 = P.op("t", lambda e, SS=SS: e.matmul(
                        pM[:, :], rotpermS[:, :], arv(SS, 512), start=True, stop=True),
                        waits=(e0, ev_cload, e3), inc="s2rt")
                    e6 = P.op("v", lambda e, SS=SS: e.tensor_tensor(
                        arv(SS + 1024, 512), arv(SS, 512), cosb[:, :], ALU.mult),
                        waits=(e0, ev_cos, st_rot.get(("wr", par))), inc="s2v")
                    e7 = P.op("v", lambda e, SS=SS: e.tensor_tensor(
                        arv(SS + 1536, 512), pM[:, :], sinb[:, :], ALU.mult),
                        waits=(e5, ev_cos), inc="s2v")
                    RES["pM"] = e7
                    e8 = P.op("v", lambda e, SS=SS: e.tensor_tensor(
                        arv(SS + 1024, 512), arv(SS + 1024, 512), arv(SS + 1536, 512), ALU.add),
                        waits=(e6, e7, st_rot.get(("wr", par))), inc="s2v")
                    st_rot[("rd", par)] = e8
                    st_rot[("evac", par)] = e0
                    ev_sp = P.dma("sy", lambda e, cc=cc, tok0=tok0, spill=spill, SS=SS: e.dma_start(
                        out=spill[cc, :, tok0:tok0 + 512], in_=arv(SS + 1024, 512)),
                        waits=(e8,), inc=f"spw{par}")
                    st_rot[("wr", par)] = ev_sp
                    if cc == HPC - 1:
                        st_rot["cos_free"] = e8
                    ev_spill[W] = ev_sp
                else:
                    e0 = P.op("v", lambda e, bk=bk, SS=SS: e.tensor_copy(arv(SS, 512), bk[:, :]),
                              waits=(evm, st_rot.get(("rd", par))), inc="s2v")
                    st_rot[("evac", par)] = e0
                    evt = None
                    for j in range(4):
                        evt = P.op("t", lambda e, j=j, SS=SS: e.transpose(
                            pT[:, 128 * j:128 * (j + 1)],
                            arv(SS, 512)[:, 128 * j:128 * (j + 1)], identS[:, :]),
                            waits=(e0, ev_cload, st_rot.get("ptv")),
                            inc="s2tr" if j == 3 else None)
                    st_rot[("rd", par)] = evt
                    e1 = P.op("v", lambda e, SS=SS: e.tensor_copy(arv(SS + 1024, 512), pT[:, :]),
                              waits=(evt, st_rot.get(("wr", par))), inc="s2v")
                    st_rot["ptv"] = e1
                    ev_sp = P.dma("sy", lambda e, cc=cc, tok0=tok0, SS=SS: e.dma_start(
                        out=v_sp[cc, :, tok0:tok0 + 512], in_=arv(SS + 1024, 512)),
                        waits=(e1,), inc=f"spw{par}")
                    st_rot[("wr", par)] = ev_sp
                    ev_spill[W] = ev_sp
            ev_htg_prev_consumed = ev_last_mm
        ev_Wmm[W] = P.ev(f"{W}mm")

    E["wslq0"] = wslab_load(wq, 0, "q", ())
    E["wslq1"] = wslab_load(wq, 1, "q", ())
    do_pass("q", q_sp, True, first_mm_extra=(ev_modev,))
    E["wslk0"] = wslab_load(wk, 0, "k", (ev_Wmm["q"],))
    E["wslk1"] = wslab_load(wk, 1, "k", (ev_Wmm["q"],))
    do_pass("k", k_sp, True)

    ev = P.op("v", lambda e: e.tensor_tensor(tiny[0:1, 5:6], tiny[0:1, 0:1], tiny[0:1, 1:2], ALU.mult),
              waits=(ev_nrm_last,), inc="nrm")
    ev = P.op("v", lambda e: e.tensor_scalar(tiny[0:1, 6:7], tiny[0:1, 5:6], (1.02 ** 2) / HD, None, ALU.mult),
              waits=(ev,), inc="nrm")
    ev = P.op("sc", lambda e: e.activation(tiny[0:1, 7:8], tiny[0:1, 6:7], AF.Sqrt), waits=(ev,), inc="cbq")
    ev = P.op("v", lambda e: e.tensor_scalar(tiny[0:1, 8:9], tiny[0:1, 7:8], -1.0, None, ALU.mult),
              waits=(ev,), inc="nrm")
    ev = P.op("t", lambda e: e.matmul(pM[:, 0:1], onesrowS[:, :], tiny[0:1, 8:9], start=True, stop=True),
              waits=(ev, ev_cload, RES["pM"]), inc="cbm")
    ev_negC = P.op("v", lambda e: e.tensor_copy(negC[:, :], pM[:, 0:1]), waits=(ev,), inc="nrm")
    RES["pM"] = ev_negC

    NGC = cfg.GASH // cfg.GC
    ev_gamm = {}
    for ki in range(F):
        par = ki % 2
        ev_l = P.dma("sy", lambda e, ki=ki, par=par: e.dma_start(
            out=arf(WMG + par * cfg.GASH, cfg.GASH),
            in_=wmod_g[128 * ki:128 * (ki + 1), :]),
            waits=(ev_s1_last, ev_gamm.get(ki - 2)), inc=f"wgl{par}")
        assert NGC <= 2
        for cg in range(NGC):
            gbk = pE if cg == 0 else pF
            ev = P.op("t", lambda e, ki=ki, cg=cg, par=par, gbk=gbk: e.matmul(
                gbk[0:2, 0:cfg.GC], ctsf[:, 2 * ki:2 * ki + 2],
                arf(WMG + par * cfg.GASH, cfg.GASH)[:, cfg.GC * cg:cfg.GC * (cg + 1)],
                start=(ki == 0), stop=(ki == F - 1)),
                waits=(ev_l, ev_siluf), inc="gamm" if cg == NGC - 1 else None)
        ev_gamm[ki] = ev
    ev_gaev = None
    for cg in range(NGC):
        gbk = pE if cg == 0 else pF
        ev_gaev = P.op("v", lambda e, cg=cg, gbk=gbk: e.tensor_copy(
            modg[0:2, cfg.GC * cg:cfg.GC * (cg + 1)], gbk[0:2, 0:cfg.GC]),
            waits=(ev_gamm[F - 1],), inc="gaev")
    ev_gast = P.dma("sy", lambda e: e.dma_start(out=agg_i.ap(), in_=modg[0:2, :]),
                    waits=(ev_gaev,), inc="gast")
    ev_agg = P.op("g", lambda e: e.collective_compute(
        "AllGather", ALU.bypass, replica_groups=[list(range(NC))],
        ins=[agg_i.ap().opt()], outs=[agg_o.ap().opt()]),
        waits=(ev_gast,), inc="agg", amt=1, kind="d")

    E["wslv0"] = wslab_load(wv, 0, "v", (ev_Wmm["k"],))
    E["wslv1"] = wslab_load(wv, 1, "v", (ev_Wmm["k"],))
    st_rot["ptv"] = ev_pt_free
    do_pass("v", v_sp, False)
    ev_spw0, ev_spw1 = P.ev("spw0"), P.ev("spw1")

    WCST = ATTS + 1024 + 2 * F * 512
    ev_wc = {}
    ev_wcs = {}
    for ck in range(NXT):
        par = ck % 2
        w0 = (ev_ht_free, P.ev(f"hw{NTG - 1}"), ev_modmm[F - 1]) if ck < 2 else ()
        w1 = (ev_wcs[ck - 2],) if ck >= 2 else ()
        ev_wc[ck] = P.dma("g", lambda e, ck=ck, par=par: e.dma_start(
            out=arv(WCST + par * D, D), in_=wo[128 * ck:128 * (ck + 1), :]),
            waits=w0 + w1, inc=f"wcl{par}")
        ev_wcs[ck] = P.dma("sy", lambda e, ck=ck, par=par: e.dma_start(
            out=wo_bf[128 * ck:128 * (ck + 1), :], in_=arv(WCST + par * D, D)),
            waits=(ev_wc[ck],), inc=f"wcs{par}")
    ev_wcst = (P.ev("wcs0"), P.ev("wcs1"))

    ev_garb = P.dma("sy", lambda e: e.dma_start(out=modg[:, :], in_=agg_o.ap()),
                    waits=(ev_agg,), inc="garb")

    def columnize_gate(which, dst, scale):
        prev = None
        nb = D // cfg.BB
        for j in range(nb):
            col = cfg.BB * j
            r, loc = GRP * which + col // cfg.GASH, col % cfg.GASH
            evm = P.op("t", lambda e, r=r, loc=loc: e.matmul(
                pM[:, 0:cfg.BB], bselS[:, 128 * r:128 * (r + 1)],
                modg[:, loc:loc + cfg.BB], start=True, stop=True),
                waits=(ev_garb, RES["pM"], prev), inc="gcm")
            prev = P.op("v", lambda e, j=j: e.tensor_copy(
                arf(GAB, D)[:, cfg.BB * j:cfg.BB * (j + 1)], pM[:, 0:cfg.BB]),
                waits=(evm,), inc="gcv")
            RES["pM"] = prev
        evt = None
        for fi in range(F):
            evt = P.op("t", lambda e, fi=fi: e.transpose(
                pM[:, fi:fi + 1], arf(GAB, D)[0:1, 128 * fi:128 * (fi + 1)],
                identfS[0:1, 0:1]),
                waits=(prev, ev_cload, RES["pM"] if fi == 0 else None),
                inc="gct" if fi == F - 1 else None)
        ev = P.op("v", lambda e: e.tensor_copy(sm[:, 0:F], pM[:, 0:F]), waits=(evt,), inc="gcv")
        RES["pM"] = ev
        ev = P.op("v", lambda e: e.tensor_scalar(sm2[:, 0:F], sm[:, 0:F], 0.0, 2.0, ALU.is_ge, ALU.mult),
                  waits=(ev,), inc="gcv")
        ev = P.op("v", lambda e: e.tensor_scalar(sm2[:, 0:F], sm2[:, 0:F], -1.0, None, ALU.add),
                  waits=(ev,), inc="gcv")
        ev = P.op("v", lambda e: e.tensor_scalar(sm[:, F:2 * F], sm[:, 0:F], -1.0, None, ALU.mult),
                  waits=(ev,), inc="gcv")
        ev = P.op("v", lambda e: e.tensor_tensor(sm[:, 0:F], sm[:, 0:F], sm[:, F:2 * F], ALU.max),
                  waits=(ev,), inc="gcv")
        ev = P.op("v", lambda e: e.tensor_scalar(sm[:, 0:F], sm[:, 0:F], 0.25, float(scale), ALU.max, ALU.mult),
                  waits=(ev,), inc="gcv")
        ev = P.op("v", lambda e, dst=dst: e.tensor_tensor(dst[:, 0:F], sm[:, 0:F], sm2[:, 0:F], ALU.mult),
                  waits=(ev,), inc="gcv")
        return ev

    ev_ga = columnize_gate(0, gacol, 1.0)
    ev_gb = columnize_gate(1, gb4col, LORA_SCALE)

    ev_cbpb = bcast_gate(4, capb, True, extra_wait=ev_s1_last)
    ev_sbb = bcast_gate(3, sab, False, extra_wait=ev_s1_last)

    wol_ev = {}

    def emit_wol(gi, wait):
        par = gi % 2
        og, ocg = divmod(gi, ND512)
        evg = evs = None
        for fi in range(F):
            eng = "g" if fi % 2 == 0 else "sy"
            ev = P.dma(eng, lambda e, ocg=ocg, fi=fi, par=par: e.dma_start(
                out=arv(WOSL + par * F * 512 + fi * 512, 512),
                in_=wo_bf[128 * fi:128 * (fi + 1), 512 * ocg:512 * (ocg + 1)]),
                waits=wait + ev_wcst,
                inc=(f"wolg{par}" if fi % 2 == 0 else f"wols{par}"))
            if fi % 2 == 0:
                evg = ev
            else:
                evs = ev
        wol_ev[gi] = (evg, evs)

    emit_wol(0, (ev_Wmm["v"], ev_spw0, ev_spw1))
    emit_wol(1, (ev_Wmm["v"], ev_spw0, ev_spw1))

    sc_banks = [pA, pB, pC]
    pv_banks = [pD, pM]
    sm_banks = [pE, pF]
    pvname = ["pD", "pM"]
    smname = ["pE", "pF"]
    bankres = {"pD": ev_modev_all, "pM": RES["pM"], "pE": ev_gaev, "pF": ev_gaev}
    exp_by_bank = {0: st_rot.get(("evac", 0)), 1: st_rot.get(("evac", 1)),
                   2: ev_modev_all}
    items = [(qg, h) for qg in range(NQG) for h in range(HPC)]
    flat = []
    for ii, (qg, h) in enumerate(items):
        nkb = (qg + 1) * TOKG // 128
        for kb in range(nkb):
            flat.append((ii, qg, h, kb, nkb))
    NFL = len(flat)
    ev_kl, ev_vl, ev_ql = {}, {}, {}
    prev_kv = {}
    prev_store = {}
    slot_busy = {}
    probs_ev = {}
    smm_ev = {}
    pv_ev = {}
    ev_aga = {}

    def emit_loads(ii):
        qg, h = items[ii]
        L = (qg + 1) * TOKG
        par = ii % 2
        gate = (ev_spw0, ev_spw1, prev_kv.get(par))
        ev_kl[ii] = P.dma("sy", lambda e, h=h, L=L, o=KBUF + par * S: e.dma_start(
            out=arv(o, L), in_=k_sp[h, :, 0:L]), waits=gate, inc=f"kl{par}")
        ev_vl[ii] = P.dma("sy", lambda e, h=h, L=L, o=VBUF + par * S: e.dma_start(
            out=arv(o, L), in_=v_sp[h, :, 0:L]), waits=gate, inc=f"vl{par}")
        ev_ql[ii] = P.dma("sy", lambda e, h=h, qg=qg, o=QBUF + par * 512: e.dma_start(
            out=arv(o, 512), in_=q_sp[h, :, TOKG * qg:TOKG * (qg + 1)]),
            waits=gate, inc=f"ql{par}")

    def emit_scores(j):
        ii, qg, h, kb, nkb = flat[j]
        par = ii % 2
        bk = sc_banks[j % 3]
        slot = PRB + 512 * (j % 4)
        ev = P.op("t", lambda e, bk=bk, kb=kb, par=par: e.matmul(
            bk[:, :], arv(KBUF + par * S + 128 * kb, 128),
            arv(QBUF + par * 512, 512), start=True, stop=True),
            waits=(ev_kl[ii], ev_ql[ii], exp_by_bank.get(j % 3)), inc="scm")
        ev_e = P.op("sc", lambda e, bk=bk, slot=slot: e.activation(
            arv(slot, 512), bk[:, :], AF.Exp, bias=negC[:, :],
            scale=1.0 / math.sqrt(HD)),
            waits=(ev, ev_negC, slot_busy.get(j % 4)), inc="exp")
        exp_by_bank[j % 3] = ev_e
        dj = kb - 4 * qg
        if dj >= 0:
            ev_e = P.op("v", lambda e, slot=slot, dj=dj: e.tensor_tensor(
                arv(slot, 512), arv(slot, 512),
                masksS[:, TOKG * dj:TOKG * (dj + 1)], ALU.mult),
                waits=(ev_e, ev_cload), inc="msk")
        probs_ev[j] = ev_e

    def emit_mmpv(j):
        ii, qg, h, kb, nkb = flat[j]
        par = ii % 2
        slot = PRB + 512 * (j % 4)
        smbk = sm_banks[ii % 2]
        pvbk = pv_banks[ii % 2]
        w0 = (bankres[smname[ii % 2]],) if kb == 0 else ()
        ev_m = P.op("t", lambda e, smbk=smbk, slot=slot, kb=kb, nkb=nkb: e.matmul(
            smbk[:, :], ones128S[:, :], arv(slot, 512),
            start=(kb == 0), stop=(kb == nkb - 1)),
            waits=(probs_ev[j], ev_cload) + w0,
            inc="smm" if kb == nkb - 1 else None)
        w1 = (bankres[pvname[ii % 2]],) if kb == 0 else ()
        ev_p = P.op("t", lambda e, pvbk=pvbk, slot=slot, kb=kb, nkb=nkb, par=par: e.matmul(
            pvbk[:, :], arv(VBUF + par * S + 128 * kb, 128), arv(slot, 512),
            start=(kb == 0), stop=(kb == nkb - 1)),
            waits=(probs_ev[j], ev_vl[ii]) + w1, inc="pvm")
        slot_busy[j % 4] = ev_p
        if kb == nkb - 1:
            smm_ev[ii] = ev_m
            pv_ev[ii] = ev_p

    def emit_it_tail(ii):
        qg, h = items[ii]
        par = ii % 2
        smbk = sm_banks[ii % 2]
        pvbk = pv_banks[ii % 2]
        ev_rc = P.op("v", lambda e, smbk=smbk: e.reciprocal(sm[:, :], smbk[:, :]),
                     waits=(smm_ev[ii],), inc="rcp")
        bankres[smname[ii % 2]] = ev_rc
        ev_div = P.op("v", lambda e, pvbk=pvbk, par=par: e.tensor_tensor(
            arv(ATTS + 512 * par, 512), pvbk[:, :], sm[:, :], ALU.mult),
            waits=(ev_rc, pv_ev[ii], prev_store.get(par)), inc="div")
        bankres[pvname[ii % 2]] = ev_div
        prev_store[par] = P.dma("sy", lambda e, qg=qg, h=h, par=par: e.dma_start(
            out=aga_i.ap()[qg, 128 * h:128 * (h + 1), :],
            in_=arv(ATTS + 512 * par, 512)),
            waits=(ev_div,), inc=f"aw{qg}")
        prev_kv[par] = pv_ev[ii]
        if h == HPC - 1:
            ev_aga[qg] = P.op("g", lambda e, qg=qg: e.collective_compute(
                "AllGather", ALU.bypass,
                replica_groups=[list(range(GRP)), list(range(GRP, 2 * GRP))],
                ins=[aga_i.ap()[qg].opt()], outs=[aga_o.ap()[qg].opt()]),
                waits=(P.ev(f"aw{qg}"),), inc="aga", amt=1, kind="d")
        if ii + 2 < len(items):
            emit_loads(ii + 2)

    emit_loads(0)
    emit_loads(1)
    emit_scores(0)
    emit_scores(1)
    for j in range(NFL):
        if j + 2 < NFL:
            emit_scores(j + 2)
        emit_mmpv(j)
        ii, qg, h, kb, nkb = flat[j]
        if kb == nkb - 1:
            emit_it_tail(ii)
    ev_aga_all = P.ev("aga")
    RES["pM"] = bankres["pM"]
    attn_pD = bankres["pD"]
    attn_pE = bankres["pE"]

    aga_flat = aga_o.ap().rearrange("a b c -> (a b) c")
    ev_woc_prev = None
    fill_hist = {}
    ev_x2w = []
    ev_attg_free = None
    ev_ep_pt = st_rot.get("ptv")
    for og in range(NOG):
        ev_gath = None
        for fi in range(F):
            ev_gath = P.op("g", lambda e, og=og, fi=fi: e.indirect_dma_start(
                arv(ATTG + fi * 512, 512), None, aga_flat,
                IndirectOffsetOnAxis(ap=idxb[:, og * F + fi:og * F + fi + 1], axis=0)),
                waits=(ev_aga_all, ev_idx, ev_attg_free), inc="agard", amt=16, kind="d")
        for ocg in range(ND512):
            gi = og * ND512 + ocg
            par = gi % 2
            if gi not in wol_ev:
                emit_wol(gi, (fill_hist.get(gi - 2),))
            ev_wg, ev_ws = wol_ev[gi]
            ev_fill = {}
            for oc in range(4):
                bk = banks6[oc]
                evf = None
                for fi in range(F):
                    evf = P.op("t", lambda e, fi=fi, oc=oc, par=par, bk=bk: e.matmul(
                        bk[:, :], arv(WOSL + par * F * 512 + fi * 512 + 128 * oc, 128),
                        arv(ATTG + fi * 512, 512),
                        start=(fi == 0), stop=(fi == F - 1)),
                        waits=(ev_wg, ev_ws, ev_gath, ev_x2w[-1] if ev_x2w else None),
                        inc="wocm" if fi == F - 1 else None)
                ev_fill[oc] = evf
            ev_woc_prev = ev_fill[3]
            fill_hist[gi] = ev_fill[3]
            ev_tt = {}
            for oc in range(4):
                gidx = ocg * 4 + oc
                e0 = P.op("v", lambda e, oc=oc, gidx=gidx: e.tensor_scalar(
                    arv(X2STG + 2048 + 512 * (oc % 2), 512), banks6[oc][:, :],
                    gacol[:, gidx:gidx + 1], None, ALU.mult),
                    waits=(ev_fill[oc], ev_ga, ev_tt.get(oc - 2)) + ev_wcst,
                    inc="epv")
                evt = None
                for j in range(4):
                    evt = P.op("t", lambda e, j=j, oc=oc: e.transpose(
                        pT[:, 128 * j:128 * (j + 1)],
                        arv(X2STG + 2048 + 512 * (oc % 2), 512)[:, 128 * j:128 * (j + 1)],
                        identS[:, :]),
                        waits=(e0, ev_ep_pt), inc="eptr" if j == 3 else None)
                exl = P.dma("sy", lambda e, og=og, gidx=gidx, oc=oc: e.dma_start(
                    out=arf(XS + 512 * (oc % 2), 512).rearrange("p (j c) -> p j c", j=4),
                    in_=xown[og * 512:(og + 1) * 512, 128 * gidx:128 * (gidx + 1)]
                    .rearrange("(j p) c -> p j c", p=128)),
                    waits=(ev_tt.get(oc - 2),), inc=f"exl{oc % 2}")
                ev2 = P.op("v", lambda e, evt=evt, oc=oc: e.tensor_tensor(
                    arv(X2STG + 512 * oc, 512), pT[:, :],
                    arf(XS + 512 * (oc % 2), 512), ALU.add),
                    waits=(evt, exl, ev_x2w[-1] if ev_x2w else None), inc="x2v")
                ev_ep_pt = ev2
                ev_tt[oc] = ev2
            for j in range(4):
                ev_x2w.append(P.dma("sy", lambda e, ocg=ocg, og=og, j=j: e.dma_start(
                    out=x2_hb[og * 512 + 128 * j:og * 512 + 128 * (j + 1),
                              512 * ocg:512 * (ocg + 1)]
                    .rearrange("p (oc c) -> p oc c", oc=4),
                    in_=arv(X2STG, 2048).rearrange("p (oc j c) -> p j oc c", oc=4, j=4)[:, j]),
                    waits=(ev_tt[3],), inc="x2w"))
        ev_attg_free = ev_fill[3]
    ev_x2done = P.ev("x2w")

    ev_ffnw = None
    for fi in range(F):
        ev_ffnw = P.dma("g", lambda e, fi=fi: e.dma_start(
            out=arv(FFNW + fi * 64, 64), in_=Apk[128 * fi:128 * (fi + 1), :]),
            waits=(ev_woc_prev,), inc="ffnw")
        ev_ffnw = P.dma("g", lambda e, fi=fi: e.dma_start(
            out=arv(FFNW + F * 64 + fi * 4, 4), in_=routerT[128 * fi:128 * (fi + 1), :]),
            waits=(ev_woc_prev,), inc="ffnw")
    ev_ffnw = P.dma("g", lambda e: e.dma_start(out=BpkS[0:64, :], in_=Bpk),
                    waits=(ev_woc_prev,), inc="ffnw")
    ev_ffnw = P.ev("ffnw")

    ev_ffnd = {}
    ev_h2c_prev = ev_ep_pt
    ev_ln_prev = None
    ev_h2r_free = None
    ev_mid_prev = None
    ev_gex_prev = None
    ev_out_prev = None
    for tt in range(cfg.NOT):
        par = tt % 2
        ev_x2 = P.dma("sy", lambda e, tt=tt, par=par: e.dma_start(
            out=arv(X2T + par * D, D), in_=x2_hb[128 * tt:128 * (tt + 1), :]),
            waits=(ev_x2done, ev_ffnd.get(tt - 2)), inc=f"x2l{par}")
        ev_bs = None
        for i in range(ND512):
            ev_bs = P.op("v", lambda e, i=i, par=par: e.bn_stats(
                stats[:, 6 * i:6 * i + 6], arv(X2T + par * D, D)[:, 512 * i:512 * (i + 1)]),
                waits=(ev_x2, ev_ln_prev if i == 0 else None), inc="l2v")
        ev_ag = P.op("v", lambda e: e.bn_aggr(mv[:, 0:2], stats[:, :]), waits=(ev_bs,), inc="l2v")
        ev_ep = P.op("v", lambda e: e.tensor_scalar(mv[:, 2:3], mv[:, 1:2], LN_EPS, None, ALU.add),
                     waits=(ev_ag,), inc="l2v")
        ev_v = P.op("v", lambda e: e.tensor_scalar(mv[:, 3:4], mv[:, 0:1], -1.0, None, ALU.mult),
                    waits=(ev_ag,), inc="l2v")
        ev_sq = P.op("sc", lambda e: e.activation(mv[:, 4:5], mv[:, 2:3], AF.Sqrt), waits=(ev_v, ev_ep), inc="l2sq")
        ev_r = P.op("v", lambda e: e.reciprocal(mv[:, 5:6], mv[:, 4:5]), waits=(ev_sq,), inc="l2v")
        ev1 = P.op("v", lambda e, par=par: e.tensor_scalar(
            arv(H2R, D), arv(X2T + par * D, D), mv[:, 3:4], mv[:, 5:6], ALU.add, ALU.mult),
            waits=(ev_r, ev_h2r_free), inc="l2v")
        ev_ln_prev = ev1
        ev2 = P.op("v", lambda e: e.tensor_tensor(arv(H2R, D), arv(H2R, D), capb[:, :], ALU.mult),
                   waits=(ev1, ev_cbpb), inc="l2v")
        ev3 = P.op("v", lambda e: e.tensor_tensor(arv(H2R, D), arv(H2R, D), sab[:, :], ALU.add),
                   waits=(ev2, ev_sbb), inc="l2v")
        ev_h2 = None
        for g4 in range(F // 4):
            evt = None
            for j in range(4):
                fi = g4 * 4 + j
                evt = P.op("t", lambda e, fi=fi, j=j: e.transpose(
                    pT[:, 128 * j:128 * (j + 1)], arv(H2R, D)[:, 128 * fi:128 * (fi + 1)],
                    identS[:, :]),
                    waits=(ev3, ev_h2c_prev), inc="h2t" if j == 3 else None)
            ev_h2 = P.op("v", lambda e, g4=g4: e.tensor_copy(arv(H2C + 512 * g4, 512), pT[:, :]),
                         waits=(evt, ev_mid_prev), inc="h2c")
            ev_h2c_prev = ev_h2
            ev_h2r_free = evt

        def h2chunk(fi):
            return arv(H2C + 128 * fi, 128)

        evl = None
        for fi in range(F):
            evl = P.op("t", lambda e, fi=fi: e.matmul(
                pM[0:4, 0:128], arv(FFNW + F * 64 + fi * 4, 4), h2chunk(fi),
                start=(fi == 0), stop=(fi == F - 1)),
                waits=(ev_h2, ev_ffnw, RES["pM"] if fi == 0 else None),
                inc="lgm" if fi == F - 1 else None)
        ev_lg = P.op("v", lambda e: e.tensor_copy(gat[0:4, :], pM[0:4, 0:128]),
                     waits=(evl,), inc="lgv")
        ev_lt = P.op("t", lambda e: e.transpose(pM[:, 8:12], gat[0:4, :], identfS[0:4, 0:4]),
                     waits=(ev_lg,), inc="lgt")
        ev = P.op("v", lambda e: e.tensor_copy(lgt[:, 0:4], pM[:, 8:12]), waits=(ev_lt,), inc="gv")
        RES["pM"] = ev
        ev = P.op("v", lambda e: e.tensor_tensor(lgt[:, 4:5], lgt[:, 0:1], lgt[:, 1:2], ALU.max), waits=(ev,), inc="gv")
        ev = P.op("v", lambda e: e.tensor_tensor(lgt[:, 6:7], lgt[:, 2:3], lgt[:, 3:4], ALU.max), waits=(ev,), inc="gv")
        ev = P.op("v", lambda e: e.tensor_tensor(lgt[:, 5:6], lgt[:, 0:1], lgt[:, 1:2], ALU.min), waits=(ev,), inc="gv")
        ev = P.op("v", lambda e: e.tensor_tensor(lgt[:, 7:8], lgt[:, 2:3], lgt[:, 3:4], ALU.min), waits=(ev,), inc="gv")
        ev = P.op("v", lambda e: e.tensor_tensor(lgt[:, 8:9], lgt[:, 4:5], lgt[:, 6:7], ALU.max), waits=(ev,), inc="gv")
        ev = P.op("v", lambda e: e.tensor_tensor(lgt[:, 9:10], lgt[:, 4:5], lgt[:, 6:7], ALU.min), waits=(ev,), inc="gv")
        ev = P.op("v", lambda e: e.tensor_tensor(lgt[:, 10:11], lgt[:, 5:6], lgt[:, 7:8], ALU.max), waits=(ev,), inc="gv")
        ev = P.op("v", lambda e: e.tensor_tensor(lgt[:, 11:12], lgt[:, 9:10], lgt[:, 10:11], ALU.max), waits=(ev,), inc="gv")
        ev = P.op("v", lambda e: e.tensor_scalar(mv[:, 6:7], lgt[:, 8:9], -1.0, None, ALU.mult), waits=(ev,), inc="gv")
        ev_e = P.op("sc", lambda e: e.activation(lgt[:, 12:16], lgt[:, 0:4], AF.Exp, bias=mv[:, 6:7]),
                    waits=(ev,), inc="gse")
        ev = P.op("v", lambda e: e.tensor_scalar(lgt[:, 0:4], lgt[:, 0:4], lgt[:, 11:12], None, ALU.is_ge),
                  waits=(ev_e,), inc="gv")
        ev = P.op("v", lambda e: e.tensor_tensor(lgt[:, 12:16], lgt[:, 12:16], lgt[:, 0:4], ALU.mult),
                  waits=(ev,), inc="gv")
        ev = P.op("v", lambda e: e.reduce_sum(mv[:, 7:8], lgt[:, 12:16], axis=AX.X), waits=(ev,), inc="gv")
        ev = P.op("v", lambda e: e.reciprocal(mv[:, 6:7], mv[:, 7:8]), waits=(ev,), inc="gv")
        ev = P.op("v", lambda e: e.tensor_scalar(lgt[:, 12:16], lgt[:, 12:16], mv[:, 6:7], None, ALU.mult),
                  waits=(ev,), inc="gv")
        ev_gt = P.op("t", lambda e: e.transpose(pM[0:4, 16:144], lgt[:, 12:16], identfS[:, :]),
                     waits=(ev, RES["pM"]), inc="gtt")
        ev_gt = P.op("v", lambda e: e.tensor_copy(gat[0:4, :], pM[0:4, 16:144]), waits=(ev_gt,), inc="gv")
        RES["pM"] = ev_gt
        ev_ge = P.op("t", lambda e: e.matmul(pE[0:64, 0:128], expandS[:, :], gat[0:4, :],
                                             start=True, stop=True),
                     waits=(ev_gt, ev_cload, ev_gex_prev, attn_pE), inc="gexm")
        ev_ge = P.op("v", lambda e: e.tensor_copy(sm[0:64, 0:128], pE[0:64, 0:128]),
                     waits=(ev_ge,), inc="gexv")
        ev_gex_prev = ev_ge
        evm = None
        for fi in range(F):
            evm = P.op("t", lambda e, fi=fi: e.matmul(
                pD[0:64, 0:128], arv(FFNW + fi * 64, 64), h2chunk(fi),
                start=(fi == 0), stop=(fi == F - 1)),
                waits=(ev_h2, ev_ffnw, ev_mid_prev, attn_pD),
                inc="midm" if fi == F - 1 else None)
        ev_mid = P.op("v", lambda e: e.tensor_tensor(
            arv(MIDB, 128)[0:64, :], pD[0:64, 0:128], sm[0:64, 0:128], ALU.mult),
            waits=(evm, ev_ge), inc="midv")
        ev_mid_prev = ev_mid
        ev_dtr_prev = None
        for oc in range(F):
            ev_d = P.op("t", lambda e, oc=oc: e.matmul(
                pC[:, 0:128], BpkS[0:64, 128 * oc:128 * (oc + 1)], arv(MIDB, 128)[0:64, :],
                start=True, stop=True),
                waits=(ev_mid, ev_ffnw, ev_dtr_prev), inc="dm")
            ev_d2 = P.op("v", lambda e, oc=oc: e.tensor_scalar(
                arv(MIDB + 128, 128), pC[:, 0:128], gb4col[:, oc:oc + 1], None, ALU.mult),
                waits=(ev_d, ev_gb, ev_dtr_prev), inc="dts")
            ev_d3 = P.op("t", lambda e: e.transpose(pT[:, 0:128], arv(MIDB + 128, 128), identS[:, :]),
                         waits=(ev_d2, ev_out_prev), inc="dtr")
            ev_dtr_prev = ev_d3
            ev_d4 = P.op("v", lambda e, oc=oc, par=par: e.tensor_tensor(
                arf(OUTS, D)[:, 128 * oc:128 * (oc + 1)], pT[:, 0:128],
                arv(X2T + par * D, D)[:, 128 * oc:128 * (oc + 1)], ALU.add),
                waits=(ev_d3, ev_x2) + ((P.ev("outw"),) if oc == 0 else ()),
                inc="outv")
            ev_out_prev = ev_d4
        ev_ffnd[tt] = ev_out_prev
        P.dma("sy", lambda e, tt=tt: e.dma_start(
            out=OUT[128 * tt:128 * (tt + 1), :], in_=arf(OUTS, D)),
            waits=(ev_out_prev,), inc="outw")

    P.op("sy", None, waits=(P.ev("outw"),))
    P.emit(nc, ctx)
    return nc, ctx


_BUILD_CACHE = {}


def get_program(key):
    if key not in _BUILD_CACHE:
        cfg = CFG(*key)
        nc, _ = build(cfg)
        _BUILD_CACHE[key] = (nc, cfg)
    return _BUILD_CACHE[key]


def kernel(L0_attn_hidden_states, L0_attn_wq, L0_attn_wk, L0_attn_wv,
           L0_attn_wo, L0_attn_kq1, L0_attn_kq2, L0_attn_kk1, L0_attn_kk2,
           L0_attn_scalingD, L0_attn_interaction_k, L0_attn_scalingD2,
           L0_ffn_router_w, L0_ffn_A, L0_ffn_B, L0_c, L0_wmod):
    x = np.asarray(L0_attn_hidden_states, dtype=np.float32)
    B, S, D = x.shape
    nc, cfg = get_program((D, S))
    T = dict(x=x, wq=np.asarray(L0_attn_wq, np.float32),
             wk=np.asarray(L0_attn_wk, np.float32),
             wv=np.asarray(L0_attn_wv, np.float32),
             wo=np.asarray(L0_attn_wo, np.float32),
             wmod=np.asarray(L0_wmod, np.float32),
             c=np.asarray(L0_c, np.float32),
             router_w=np.asarray(L0_ffn_router_w, np.float32),
             A=np.asarray(L0_ffn_A, np.float32),
             B=np.asarray(L0_ffn_B, np.float32))
    consts = host_constants(cfg)
    in_maps = [host_core_inputs(cfg, c_, T, consts) for c_ in range(NC)]
    res = run_bass_kernel_spmd(nc, in_maps, core_ids=list(range(NC)))
    out = np.empty((B, S, D), dtype=np.float32)
    for c_ in range(NC):
        b, p = c_ // GRP, c_ % GRP
        out[b, cfg.OWN * p: cfg.OWN * (p + 1)] = res.results[c_]["out"]
    return out


# version 11 (final)
# speedup vs baseline: 1.4134x; 1.0674x over previous
import math
import contextlib
import numpy as np
import ml_dtypes
import concourse.bass as bass
import concourse.mybir as mybir
from concourse.bass_utils import run_bass_kernel_spmd
from concourse.bass import IndirectOffsetOnAxis

F32 = mybir.dt.float32
BF16 = mybir.dt.bfloat16
U32 = mybir.dt.uint32
AF = mybir.ActivationFunctionType
ALU = mybir.AluOpType
AX = mybir.AxisListType
BF = ml_dtypes.bfloat16

NC = 8
GRP = 4
HD = 128
TOKG = 512
LN_EPS = 1e-5
LORA_SCALE = 4.0


class CFG:
    def __init__(self, D=4096, S=4096):
        self.D, self.S = D, S
        self.F = D // 128
        self.H = D // HD
        self.HPC = self.H // GRP
        self.WC = self.HPC * HD
        self.NTG = S // TOKG
        self.NQG = S // TOKG
        self.OWN = S // GRP
        self.NOT = self.OWN // 128
        self.NOG = self.OWN // TOKG
        self.MODSH = 6 * D // NC
        self.GASH = D // GRP
        self.BB = 512 if self.MODSH % 512 == 0 else 128
        self.GC = min(512, self.GASH)
        assert S % TOKG == 0 and D % 512 == 0 and self.OWN % TOKG == 0
        assert self.MODSH % self.BB == 0 and self.GASH % self.GC == 0
        assert self.F % 4 == 0 and self.HPC >= 1


class Prog:
    ENGINES = ("t", "v", "sc", "sy", "g")

    CSEM = {"t": "tC", "v": "vC", "sc": "scC", "sy": "syC", "g": "gC"}

    def __init__(self):
        self.ops = {e: [] for e in self.ENGINES}
        self.cnt = {}
        self.kind = {}
        self.owner = {}
        self.last_ev = {}

    def op(self, eng, fn, waits=(), inc=None, amt=1, kind="c"):
        ev = None
        if inc is not None:
            sem = self.CSEM[eng] if kind == "c" else inc
            if sem in self.owner:
                assert self.owner[sem] == eng and self.kind[sem] == kind, sem
            self.owner[sem] = eng
            self.kind[sem] = kind
            self.cnt[sem] = self.cnt.get(sem, 0) + amt
            ev = (sem, self.cnt[sem])
            self.last_ev[inc] = ev
        ws = tuple(w for w in waits if w is not None and w[1] > 0)
        self.ops[eng].append((ws, fn, ev[0] if ev else None, amt))
        return ev

    def dma(self, eng, fn, waits=(), inc=None):
        return self.op(eng, fn, waits, inc, 16, "d")

    def ev(self, name):
        if name in self.cnt:
            return (name, self.cnt[name])
        return self.last_ev.get(name)

    def emit(self, nc, ctx):
        sems = {}
        for name in self.cnt:
            sems[name] = ctx.enter_context(nc.semaphore(name))
        block = ctx.enter_context(nc.Block())

        def section(eng_name):
            def body(eng):
                seen = {}
                for ws, fn, inc, amt in self.ops[eng_name]:
                    for sem, val in ws:
                        if seen.get(sem, -1) >= val:
                            continue
                        eng.wait_ge(sems[sem], val)
                        seen[sem] = val
                    if fn is None:
                        continue
                    ins = fn(eng)
                    if inc is not None:
                        ins.then_inc(sems[inc], amt)
            return body

        block.tensor(section("t"))
        block.vector(section("v"))
        block.scalar(section("sc"))
        block.sync(section("sy"))
        block.gpsimd(section("g"))


def host_constants(cfg):
    D, S = cfg.D, cfg.S
    c = {}
    c["ident"] = np.eye(128, dtype=BF)
    c["identf"] = np.eye(128, dtype=np.float32)
    c["ones128"] = np.ones((128, 128), dtype=BF)
    c["onesrow"] = np.ones((1, 128), dtype=np.float32)
    P = np.zeros((128, 128), dtype=np.float32)
    for p in range(64):
        P[p + 64, p] = -1.0
        P[p, p + 64] = 1.0
    c["rotperm"] = P.astype(BF)
    inv = 1.0 / 10000.0 ** (np.arange(0, HD, 2, dtype=np.float64) / HD)
    invf = np.concatenate([inv, inv])
    ang = invf[:, None] * np.arange(S, dtype=np.float64)[None, :]
    c["cosT"] = np.cos(ang).astype(BF)
    c["sinT"] = np.sin(ang).astype(BF)
    nj = TOKG // 128
    m = np.zeros((128, nj * TOKG), dtype=BF)
    for j in range(nj):
        kp = np.arange(128)[:, None] + 128 * j
        qf = np.arange(TOKG)[None, :]
        m[:, j * TOKG:(j + 1) * TOKG] = (kp <= qf).astype(BF)
    c["masks"] = m
    c["expand"] = np.repeat(np.eye(4, dtype=np.float32), 16, axis=1)
    c["iota"] = np.arange(128, dtype=np.float32).reshape(128, 1)
    return c


def host_core_inputs(cfg, core, T, consts):
    D = cfg.D
    b, p = core // GRP, core % GRP
    x = np.ascontiguousarray(T["x"][b])
    wsl = slice(cfg.WC * p, cfg.WC * (p + 1))
    d = dict(consts)
    d["xb"] = x
    d["xown"] = np.ascontiguousarray(x[cfg.OWN * p: cfg.OWN * (p + 1)])
    d["wq"] = np.ascontiguousarray(T["wq"][:, wsl])
    d["wk"] = np.ascontiguousarray(T["wk"][:, wsl])
    d["wv"] = np.ascontiguousarray(T["wv"][:, wsl])
    d["wo"] = T["wo"]
    d["wmod_s"] = np.ascontiguousarray(
        T["wmod"][:, cfg.MODSH * core: cfg.MODSH * (core + 1)])
    gbase = (2 * D if core < GRP else 5 * D) + cfg.GASH * (core % GRP)
    d["wmod_g"] = np.ascontiguousarray(T["wmod"][:, gbase: gbase + cfg.GASH])
    d["cT"] = np.ascontiguousarray(T["c"].T)
    d["routerT"] = np.ascontiguousarray(T["router_w"].T)
    d["Apk"] = np.ascontiguousarray(T["A"].transpose(2, 0, 1).reshape(D, 64))
    d["Bpk"] = np.ascontiguousarray(T["B"].transpose(0, 2, 1).reshape(64, D))
    bs = np.zeros((2 * NC, NC * 128), dtype=np.float32)
    for r in range(NC):
        bs[2 * r + b, 128 * r: 128 * (r + 1)] = 1.0
    d["bsel16"] = bs
    d["baseq"] = np.stack(
        [np.full(128, float((cfg.NOG * p + k) * D), dtype=np.float32)
         for k in range(cfg.NOG)], axis=1)
    return d


def build(cfg):
    D, S, F, HPC, WC = cfg.D, cfg.S, cfg.F, cfg.HPC, cfg.WC
    NTG, NQG, OWN, NOG = cfg.NTG, cfg.NQG, cfg.OWN, cfg.NOG
    NJ = TOKG // 128
    ND512 = D // 512
    nc = bass.Bass("TRN2", target_bir_lowering=False, debug=False)
    ctx = contextlib.ExitStack()

    def din(name, shape, dt=F32):
        return nc.dram_tensor(name, shape, dt, kind="ExternalInput").ap()

    xb = din("xb", [S, D]); xown = din("xown", [OWN, D])
    wq = din("wq", [D, WC]); wk = din("wk", [D, WC]); wv = din("wv", [D, WC])
    wo = din("wo", [D, D])
    wmod_s = din("wmod_s", [D, cfg.MODSH]); wmod_g = din("wmod_g", [D, cfg.GASH])
    cT = din("cT", [D, 2])
    routerT = din("routerT", [D, 4]); Apk = din("Apk", [D, 64]); Bpk = din("Bpk", [64, D])
    ident = din("ident", [128, 128], BF16); identf = din("identf", [128, 128], F32)
    ones128 = din("ones128", [128, 128], BF16); onesrow = din("onesrow", [1, 128], F32)
    rotperm = din("rotperm", [128, 128], BF16)
    cosT = din("cosT", [128, S], BF16); sinT = din("sinT", [128, S], BF16)
    masks = din("masks", [128, NJ * TOKG], BF16)
    expand = din("expand", [4, 64], F32); iota = din("iota", [128, 1], F32)
    bsel16 = din("bsel16", [2 * NC, NC * 128], F32)
    baseq = din("baseq", [128, NOG], F32)
    OUT = nc.dram_tensor("out", [OWN, D], F32, kind="ExternalOutput").ap()

    h_fm = nc.dram_tensor("h_fm", [F, 128, S], BF16).ap()
    wo_bf = nc.dram_tensor("wo_bf", [D, D], BF16).ap()
    q_sp = nc.dram_tensor("q_sp", [HPC, 128, S], BF16).ap()
    k_sp = nc.dram_tensor("k_sp", [HPC, 128, S], BF16).ap()
    v_sp = nc.dram_tensor("v_sp", [HPC, 128, S], BF16).ap()
    x2_hb = nc.dram_tensor("x2_hb", [OWN, D], BF16).ap()
    agm_i = nc.dram_tensor("agm_i", [2, cfg.MODSH], F32)
    agm_o = nc.dram_tensor("agm_o", [2 * NC, cfg.MODSH], F32)
    agg_i = nc.dram_tensor("agg_i", [2, cfg.GASH], F32)
    agg_o = nc.dram_tensor("agg_o", [2 * NC, cfg.GASH], F32)
    aga_i = nc.dram_tensor("aga_i", [NQG, WC, TOKG], BF16)
    aga_o = nc.dram_tensor("aga_o", [NQG, GRP * WC, TOKG], BF16)

    pA = ctx.enter_context(nc.psum_tensor("pA", [128, 512], F32))
    pB = ctx.enter_context(nc.psum_tensor("pB", [128, 512], F32))
    pC = ctx.enter_context(nc.psum_tensor("pC", [128, 512], F32))
    pD = ctx.enter_context(nc.psum_tensor("pD", [128, 512], F32))
    pE = ctx.enter_context(nc.psum_tensor("pE", [128, 512], F32))
    pF = ctx.enter_context(nc.psum_tensor("pF", [128, 512], F32))
    pM = ctx.enter_context(nc.psum_tensor("pM", [128, 512], F32))
    pT = ctx.enter_context(nc.psum_tensor("pT", [128, 512], BF16))
    banksAB = [pA, pB]
    banks6 = [pA, pB, pC, pD, pE, pF]

    sb = lambda n, s, d: ctx.enter_context(nc.sbuf_tensor(n, s, d))
    identS = sb("identS", [128, 128], BF16)
    identfS = sb("identfS", [128, 128], F32)
    ones128S = sb("ones128S", [128, 128], BF16)
    onesrowS = sb("onesrowS", [1, 128], F32)
    rotpermS = sb("rotpermS", [128, 128], BF16)
    masksS = sb("masksS", [128, NJ * TOKG], BF16)
    expandS = sb("expandS", [4, 64], F32)
    iotaS = sb("iotaS", [128, 1], F32)
    bselS = sb("bselS", [2 * NC, NC * 128], F32)
    baseqS = sb("baseqS", [128, NOG], F32)
    cosb = sb("cosb", [128, 512], BF16)
    sinb = sb("sinb", [128, 512], BF16)
    capb = sb("capb", [128, D], BF16)
    sab = sb("sab", [128, D], BF16)
    gacol = sb("gacol", [128, max(cfg.F, 2)], F32)
    gb4col = sb("gb4col", [128, max(cfg.F, 2)], F32)
    negC = sb("negC", [128, 1], F32)
    idxb = sb("idxb", [128, max(NOG * F, 4)], U32)
    cts = sb("cts", [128, 2 * F], F32)
    ctsb = sb("ctsb", [128, 2 * F], BF16)
    ctsf = sb("ctsf", [128, 2 * F], F32)
    stats = sb("stats", [128, 12 * ND512], F32)
    mv = sb("mv", [128, 16], F32)
    tiny = sb("tiny", [128, 16], F32)
    sm = sb("sm", [128, 512], F32)
    sm2 = sb("sm2", [128, max(cfg.F, 2)], F32)
    modsh = sb("modsh", [2 * NC, cfg.MODSH], F32)
    modg = sb("modg", [2 * NC, cfg.GASH], F32)
    lgt = sb("lgt", [128, 16], F32)
    gat = sb("gat", [4, 128], F32)

    ARN = 64512
    AR = sb("AR", [128, ARN], BF16)
    ARFN = 8192
    ARF = sb("ARF", [128, ARFN], F32)

    def arv(off, n):
        assert off + n <= ARN, (off, n)
        return AR[:, off:off + n]

    def arf(off, n):
        assert off + n <= ARFN, (off, n)
        return ARF[:, off:off + n]

    WSLAB = 0
    HTG = F * WC
    S2STG = HTG + F * 512
    HTMS1 = S2STG + 4096
    S1STG = HTMS1 + D
    WMODBUF = S1STG + 1024
    assert WMODBUF + 2 * cfg.MODSH <= ARN
    KBUF = 0
    VBUF = 2 * S
    QBUF = 4 * S
    PRB = QBUF + 1024
    ATTS = PRB + 2048
    ATTG = 0
    WOSL = ATTS + 1024
    X2STG = WOSL + 2 * F * 512
    assert X2STG + 3584 <= ARN
    H2C = F * 512
    FFNW = H2C + F * 128
    BPK = FFNW + F * 68
    MIDB = BPK + D
    X2T = MIDB + 512
    H2R = X2T + 2 * D
    assert H2R + D <= ARN
    BpkS = AR[:, BPK:BPK + D]

    XT = 0
    GAB = 0
    WMG = D
    XS = D + 2048
    OUTS = D
    assert max(2 * D, D + 2048 + 1024, GAB + D, OUTS + D) <= ARFN

    P = Prog()
    E = {}
    RES = {"pM": None}

    for dst, src in ((identS, ident), (identfS, identf), (ones128S, ones128),
                     (onesrowS, onesrow), (rotpermS, rotperm),
                     (masksS, masks), (expandS, expand),
                     (iotaS, iota), (bselS, bsel16), (baseqS, baseq)):
        P.dma("sy", lambda e, d=dst, s=src: e.dma_start(out=d[:, :], in_=s), inc="cload")
    ev_cload = P.ev("cload")

    for k in range(NOG):
        for fi in range(F):
            P.op("v", lambda e, k=k, fi=fi: e.tensor_scalar(
                idxb[:, k * F + fi: k * F + fi + 1], iotaS[:, :],
                baseqS[:, k:k + 1], float(fi * 128), ALU.add, ALU.add),
                waits=(ev_cload,), inc="idx")
    ev_idx = P.ev("idx")

    P.dma("sy", lambda e: e.dma_start(
        out=cts[:, :].rearrange("p (k b) -> p k b", k=F),
        in_=cT.rearrange("(k p) b -> p k b", p=128)), inc="ctl")
    ev_silu = P.op("sc", lambda e: e.activation(ctsb[:, :], cts[:, :], AF.Silu),
                   waits=(P.ev("ctl"),), inc="silu")
    ev_siluf = P.op("sc", lambda e: e.activation(ctsf[:, :], cts[:, :], AF.Silu),
                    waits=(P.ev("ctl"),), inc="siluf")

    NCG = cfg.MODSH // cfg.BB
    ev_modmm = {}
    for ki in range(F):
        par = ki % 2
        off = WMODBUF + par * cfg.MODSH
        ev_l = P.dma("g", lambda e, ki=ki, off=off: e.dma_start(
            out=arv(off, cfg.MODSH), in_=wmod_s[128 * ki:128 * (ki + 1), :]),
            waits=(ev_modmm.get(ki - 2),), inc=f"wml{par}")
        for cg in range(NCG):
            bk = banks6[cg % 6]
            ev = P.op("t", lambda e, ki=ki, cg=cg, off=off, bk=bk: e.matmul(
                bk[0:2, 0:cfg.BB], ctsb[:, 2 * ki:2 * ki + 2],
                arv(off, cfg.MODSH)[:, cfg.BB * cg:cfg.BB * (cg + 1)],
                start=(ki == 0), stop=(ki == F - 1)),
                waits=(ev_silu, ev_l), inc="modmm" if cg == NCG - 1 else None)
        ev_modmm[ki] = ev
    ev_modev = None
    for cg in range(NCG):
        bk = banks6[cg % 6]
        ev_modev = P.op("v", lambda e, cg=cg, bk=bk: e.tensor_copy(
            modsh[0:2, cfg.BB * cg:cfg.BB * (cg + 1)], bk[0:2, 0:cfg.BB]),
            waits=(ev_modmm[F - 1],), inc="modev")
    ev_modst = P.dma("sy", lambda e: e.dma_start(out=agm_i.ap(), in_=modsh[0:2, :]),
                     waits=(ev_modev,), inc="modst")
    ev_agm = P.op("g", lambda e: e.collective_compute(
        "AllGather", ALU.bypass, replica_groups=[list(range(NC))],
        ins=[agm_i.ap().opt()], outs=[agm_o.ap().opt()]),
        waits=(ev_modst,), inc="agm", amt=1, kind="d")
    ev_modrb = P.dma("sy", lambda e: e.dma_start(out=modsh[:, :], in_=agm_o.ap()),
                     waits=(ev_agm,), inc="modrb")

    def bcast_gate(gate_idx, dst, add1, extra_wait=None):
        prev = None
        nb = D // cfg.BB
        for j in range(nb):
            col = gate_idx * D + cfg.BB * j
            r, loc = col // cfg.MODSH, col % cfg.MODSH
            evm = P.op("t", lambda e, r=r, loc=loc: e.matmul(
                pM[:, 0:cfg.BB], bselS[:, 128 * r:128 * (r + 1)],
                modsh[:, loc:loc + cfg.BB], start=True, stop=True),
                waits=(ev_modrb, ev_cload, RES["pM"], prev, extra_wait), inc="bgm")
            if add1:
                prev = P.op("v", lambda e, j=j, dst=dst: e.tensor_scalar(
                    dst[:, cfg.BB * j:cfg.BB * (j + 1)], pM[:, 0:cfg.BB],
                    1.0, None, ALU.add), waits=(evm,), inc="bgv")
            else:
                prev = P.op("v", lambda e, j=j, dst=dst: e.tensor_copy(
                    dst[:, cfg.BB * j:cfg.BB * (j + 1)], pM[:, 0:cfg.BB]),
                    waits=(evm,), inc="bgv")
            RES["pM"] = prev
        return prev

    ev_capb = bcast_gate(1, capb, True)
    ev_sab = bcast_gate(0, sab, False)

    NXT = S // 128
    ev_s1done = {}
    ag_hist = {}
    ap_hist = {}
    ev_ht_free = None
    ev_hw = {}
    ev_pt_free = None
    ev_hst = [None, None]
    ev_s1_last = None
    for xt in range(NXT):
        par = xt % 2
        o = 6 * ND512 * par
        om = 8 * par
        tg = (xt * 128) // TOKG
        ev_xl = P.dma("sy", lambda e, xt=xt, par=par: e.dma_start(
            out=arf(XT + par * D, D), in_=xb[128 * xt:128 * (xt + 1), :]),
            waits=(ev_s1done.get(xt - 2),), inc=f"xld{par}")
        ev_bs = None
        for i in range(ND512):
            ev_bs = P.op("v", lambda e, i=i, par=par, o=o: e.bn_stats(
                stats[:, o + 6 * i:o + 6 * i + 6],
                arf(XT + par * D, D)[:, 512 * i:512 * (i + 1)]),
                waits=(ev_xl, ag_hist.get(par) if i == 0 else None), inc="s1v")
        ev_ag = P.op("v", lambda e, o=o, om=om: e.bn_aggr(
            mv[:, om:om + 2], stats[:, o:o + 6 * ND512]),
            waits=(ev_bs, ap_hist.get(par)), inc="s1v")
        ag_hist[par] = ev_ag
        ev_ep = P.op("v", lambda e, om=om: e.tensor_scalar(
            mv[:, om + 2:om + 3], mv[:, om + 1:om + 2], LN_EPS, None, ALU.add),
            waits=(ev_ag,), inc="s1v")
        ev_v = P.op("v", lambda e, om=om: e.tensor_scalar(
            mv[:, om + 3:om + 4], mv[:, om:om + 1], -1.0, None, ALU.mult),
            waits=(ev_ag,), inc="s1v")
        ev_sq = P.op("sc", lambda e, om=om: e.activation(
            mv[:, om + 4:om + 5], mv[:, om + 2:om + 3], AF.Sqrt),
            waits=(ev_v, ev_ep), inc="s1sq")
        ev_r = P.op("v", lambda e, om=om: e.reciprocal(
            mv[:, om + 5:om + 6], mv[:, om + 4:om + 5]), waits=(ev_sq,), inc="s1v")
        ev_b = P.op("v", lambda e, om=om: e.tensor_tensor(
            mv[:, om + 6:om + 7], mv[:, om + 3:om + 4], mv[:, om + 5:om + 6],
            ALU.mult), waits=(ev_r,), inc="s1v")
        ev1 = P.op("sc", lambda e, par=par, om=om: e.activation(
            arv(HTMS1, D), arf(XT + par * D, D), AF.Identity,
            bias=mv[:, om + 6:om + 7], scale=mv[:, om + 5:om + 6]),
            waits=(ev_b, ev_ht_free), inc="s1ap")
        ap_hist[par] = ev1
        ev2 = P.op("v", lambda e: e.tensor_tensor(arv(HTMS1, D), arv(HTMS1, D), capb[:, :], ALU.mult),
                   waits=(ev1, ev_capb), inc="s1v")
        ev3 = P.op("v", lambda e: e.tensor_tensor(arv(HTMS1, D), arv(HTMS1, D), sab[:, :], ALU.add),
                   waits=(ev2, ev_sab), inc="s1v")
        ev_s1done[xt] = ev3
        ev_s1_last = ev3
        for g4 in range(F // 4):
            evt = None
            for j in range(4):
                fi = g4 * 4 + j
                evt = P.op("t", lambda e, fi=fi, j=j: e.transpose(
                    pT[:, 128 * j:128 * (j + 1)],
                    arv(HTMS1, D)[:, 128 * fi:128 * (fi + 1)], identS[:, :]),
                    waits=(ev3, ev_cload, ev_pt_free), inc="s1tr" if j == 3 else None)
            ev_ht_free = evt
            sl = g4 % 2
            evc = P.op("sc", lambda e, sl=sl: e.activation(
                arv(S1STG + sl * 512, 512), pT[:, :], AF.Copy),
                waits=(evt, ev_hst[sl]), inc="s1hc")
            ev_pt_free = evc
            ev_hst[sl] = P.dma("sy", lambda e, xt=xt, g4=g4, sl=sl: e.dma_start(
                out=h_fm[4 * g4:4 * g4 + 4, :, 128 * xt:128 * (xt + 1)].rearrange("f p t -> p f t"),
                in_=arv(S1STG + sl * 512, 512).rearrange("p (f t) -> p f t", f=4)),
                waits=(evc,), inc=f"hw{tg}")
    for tg in range(NTG):
        ev_hw[tg] = P.ev(f"hw{tg}")

    HF = F // 2
    ev_modev_all = P.ev("modev")

    def wslab_load(Wap, half, tag, wait_evs):
        ev = None
        for fi in range(HF):
            f = half * HF + fi
            ev = P.dma("g", lambda e, f=f, fi=fi, half=half, Wap=Wap: e.dma_start(
                out=arv(WSLAB + half * HF * WC + fi * WC, WC),
                in_=Wap[128 * f:128 * (f + 1), :]),
                waits=wait_evs, inc=f"wsl{tag}{half}")
        return ev

    P.op("v", lambda e: e.memset(tiny[0:1, 0:2], 0.0), inc="nrm")
    st_rot = {}
    ev_nrm_last = None
    ev_spill = {"q": None, "k": None, "v": None}
    ev_Wmm = {}

    chg = {}

    def do_pass(W, spill, rot, first_mm_extra=()):
        nonlocal ev_nrm_last
        ni = 0 if W == "q" else 1
        for tg in range(NTG):
            tok0 = TOKG * tg
            ev_hc = {}
            for fi in range(F):
                ev_hc[fi] = P.dma("sy", lambda e, fi=fi, tg=tg: e.dma_start(
                    out=arv(HTG + fi * 512, 512),
                    in_=h_fm[fi, :, TOKG * tg:TOKG * (tg + 1)]),
                    waits=(ev_hw[tg], chg.get(fi // 8)), inc="htg")
            if rot:
                P.dma("sy", lambda e, tok0=tok0: e.dma_start(
                    out=cosb[:, :], in_=cosT[:, tok0:tok0 + 512]),
                    waits=(st_rot.get("cos_free"),), inc="cosl")
                P.dma("sy", lambda e, tok0=tok0: e.dma_start(
                    out=sinb[:, :], in_=sinT[:, tok0:tok0 + 512]),
                    waits=(st_rot.get("cos_free"),), inc="cosl")
                ev_cos = P.ev("cosl")
            ev_last_mm = None
            for cc in range(HPC):
                it = tg * HPC + cc
                par = it % 2
                bk = banksAB[par]
                SS = S2STG + 2048 * par
                extra = first_mm_extra if (tg * HPC + cc) < 2 else ()
                for f in range(F):
                    half = f // HF
                    lbl = f"{W}mm" if f == F - 1 else (
                        "chg" if (cc == HPC - 1 and f % 8 == 7) else None)
                    ev_last_mm = P.op("t", lambda e, f=f, cc=cc, bk=bk, half=half: e.matmul(
                        bk[:, :], arv(WSLAB + half * HF * WC + (f % HF) * WC + cc * 128, 128),
                        arv(HTG + f * 512, 512),
                        start=(f == 0), stop=(f == F - 1)),
                        waits=(ev_hc[f], E[f"wsl{W}{half}"],
                               st_rot.get(("evac", par))) + tuple(extra),
                        inc=lbl)
                    if cc == HPC - 1 and f % 8 == 7:
                        chg[f // 8] = ev_last_mm
                evm = ev_last_mm
                if rot:
                    e0 = P.op("v", lambda e, bk=bk, SS=SS: e.tensor_copy(arv(SS, 512), bk[:, :]),
                              waits=(evm, st_rot.get(("rd", par))), inc="s2v")
                    e1 = P.op("v", lambda e, SS=SS: e.tensor_tensor(
                        arv(SS + 512, 512), arv(SS, 512), arv(SS, 512), ALU.mult),
                        waits=(e0,), inc="s2v")
                    e2 = P.op("t", lambda e, SS=SS: e.matmul(
                        pM[0:1, :], ones128S[:, 0:1], arv(SS + 512, 512),
                        start=True, stop=True),
                        waits=(e1, RES["pM"]), inc="s2nt")
                    e3 = P.op("v", lambda e: e.reduce_max(tiny[0:1, 4:5], pM[0:1, :], axis=AX.X),
                              waits=(e2, ev_nrm_last), inc="s2v")
                    e4 = P.op("v", lambda e, ni=ni: e.tensor_tensor(
                        tiny[0:1, ni:ni + 1], tiny[0:1, ni:ni + 1], tiny[0:1, 4:5], ALU.max),
                        waits=(e3,), inc="nrm")
                    ev_nrm_last = e4
                    e5 = P.op("t", lambda e, SS=SS: e.matmul(
                        pM[:, :], rotpermS[:, :], arv(SS, 512), start=True, stop=True),
                        waits=(e0, ev_cload, e3), inc="s2rt")
                    e6 = P.op("v", lambda e, SS=SS: e.tensor_tensor(
                        arv(SS + 1024, 512), arv(SS, 512), cosb[:, :], ALU.mult),
                        waits=(e0, ev_cos, st_rot.get(("wr", par))), inc="s2v")
                    e7 = P.op("v", lambda e, SS=SS: e.tensor_tensor(
                        arv(SS + 1536, 512), pM[:, :], sinb[:, :], ALU.mult),
                        waits=(e5, ev_cos), inc="s2v")
                    RES["pM"] = e7
                    e8 = P.op("v", lambda e, SS=SS: e.tensor_tensor(
                        arv(SS + 1024, 512), arv(SS + 1024, 512), arv(SS + 1536, 512), ALU.add),
                        waits=(e6, e7, st_rot.get(("wr", par))), inc="s2v")
                    st_rot[("rd", par)] = e8
                    st_rot[("evac", par)] = e0
                    ev_sp = P.dma("sy", lambda e, cc=cc, tok0=tok0, spill=spill, SS=SS: e.dma_start(
                        out=spill[cc, :, tok0:tok0 + 512], in_=arv(SS + 1024, 512)),
                        waits=(e8,), inc=f"spw{par}")
                    st_rot[("wr", par)] = ev_sp
                    if cc == HPC - 1:
                        st_rot["cos_free"] = e8
                    ev_spill[W] = ev_sp
                else:
                    e0 = P.op("v", lambda e, bk=bk, SS=SS: e.tensor_copy(arv(SS, 512), bk[:, :]),
                              waits=(evm, st_rot.get(("rd", par))), inc="s2v")
                    st_rot[("evac", par)] = e0
                    evt = None
                    for j in range(4):
                        evt = P.op("t", lambda e, j=j, SS=SS: e.transpose(
                            pT[:, 128 * j:128 * (j + 1)],
                            arv(SS, 512)[:, 128 * j:128 * (j + 1)], identS[:, :]),
                            waits=(e0, ev_cload, st_rot.get("ptv")),
                            inc="s2tr" if j == 3 else None)
                    st_rot[("rd", par)] = evt
                    e1 = P.op("v", lambda e, SS=SS: e.tensor_copy(arv(SS + 1024, 512), pT[:, :]),
                              waits=(evt, st_rot.get(("wr", par))), inc="s2v")
                    st_rot["ptv"] = e1
                    ev_sp = P.dma("sy", lambda e, cc=cc, tok0=tok0, SS=SS: e.dma_start(
                        out=v_sp[cc, :, tok0:tok0 + 512], in_=arv(SS + 1024, 512)),
                        waits=(e1,), inc=f"spw{par}")
                    st_rot[("wr", par)] = ev_sp
                    ev_spill[W] = ev_sp
        ev_Wmm[W] = P.ev(f"{W}mm")

    E["wslq0"] = wslab_load(wq, 0, "q", ())
    E["wslq1"] = wslab_load(wq, 1, "q", ())
    do_pass("q", q_sp, True, first_mm_extra=(ev_modev,))
    E["wslk0"] = wslab_load(wk, 0, "k", (ev_Wmm["q"],))
    E["wslk1"] = wslab_load(wk, 1, "k", (ev_Wmm["q"],))
    do_pass("k", k_sp, True)

    ev = P.op("v", lambda e: e.tensor_tensor(tiny[0:1, 5:6], tiny[0:1, 0:1], tiny[0:1, 1:2], ALU.mult),
              waits=(ev_nrm_last,), inc="nrm")
    ev = P.op("v", lambda e: e.tensor_scalar(tiny[0:1, 6:7], tiny[0:1, 5:6], (1.02 ** 2) / HD, None, ALU.mult),
              waits=(ev,), inc="nrm")
    ev = P.op("sc", lambda e: e.activation(tiny[0:1, 7:8], tiny[0:1, 6:7], AF.Sqrt), waits=(ev,), inc="cbq")
    ev = P.op("v", lambda e: e.tensor_scalar(tiny[0:1, 8:9], tiny[0:1, 7:8], -1.0, None, ALU.mult),
              waits=(ev,), inc="nrm")
    ev = P.op("t", lambda e: e.matmul(pM[:, 0:1], onesrowS[:, :], tiny[0:1, 8:9], start=True, stop=True),
              waits=(ev, ev_cload, RES["pM"]), inc="cbm")
    ev_negC = P.op("v", lambda e: e.tensor_copy(negC[:, :], pM[:, 0:1]), waits=(ev,), inc="nrm")
    RES["pM"] = ev_negC

    NGC = cfg.GASH // cfg.GC
    ev_gamm = {}
    for ki in range(F):
        par = ki % 2
        ev_l = P.dma("sy", lambda e, ki=ki, par=par: e.dma_start(
            out=arf(WMG + par * cfg.GASH, cfg.GASH),
            in_=wmod_g[128 * ki:128 * (ki + 1), :]),
            waits=(ev_s1_last, ev_gamm.get(ki - 2)), inc=f"wgl{par}")
        assert NGC <= 2
        for cg in range(NGC):
            gbk = pE if cg == 0 else pF
            ev = P.op("t", lambda e, ki=ki, cg=cg, par=par, gbk=gbk: e.matmul(
                gbk[0:2, 0:cfg.GC], ctsf[:, 2 * ki:2 * ki + 2],
                arf(WMG + par * cfg.GASH, cfg.GASH)[:, cfg.GC * cg:cfg.GC * (cg + 1)],
                start=(ki == 0), stop=(ki == F - 1)),
                waits=(ev_l, ev_siluf), inc="gamm" if cg == NGC - 1 else None)
        ev_gamm[ki] = ev
    ev_gaev = None
    for cg in range(NGC):
        gbk = pE if cg == 0 else pF
        ev_gaev = P.op("v", lambda e, cg=cg, gbk=gbk: e.tensor_copy(
            modg[0:2, cfg.GC * cg:cfg.GC * (cg + 1)], gbk[0:2, 0:cfg.GC]),
            waits=(ev_gamm[F - 1],), inc="gaev")
    ev_gast = P.dma("sy", lambda e: e.dma_start(out=agg_i.ap(), in_=modg[0:2, :]),
                    waits=(ev_gaev,), inc="gast")

    E["wslv0"] = wslab_load(wv, 0, "v", (ev_Wmm["k"],))
    E["wslv1"] = wslab_load(wv, 1, "v", (ev_Wmm["k"],))
    ev_agg = P.op("g", lambda e: e.collective_compute(
        "AllGather", ALU.bypass, replica_groups=[list(range(NC))],
        ins=[agg_i.ap().opt()], outs=[agg_o.ap().opt()]),
        waits=(ev_gast,), inc="agg", amt=1, kind="d")
    ev_garb = P.dma("sc", lambda e: e.dma_start(out=modg[:, :], in_=agg_o.ap()),
                    waits=(ev_agg,), inc="garb")
    st_rot["ptv"] = ev_pt_free
    do_pass("v", v_sp, False)
    ev_spw0, ev_spw1 = P.ev("spw0"), P.ev("spw1")

    WCST = ATTS + 1024 + 2 * F * 512
    ev_wc = {}
    ev_wcs = {}
    for ck in range(NXT):
        par = ck % 2
        w0 = (ev_ht_free, P.ev(f"hw{NTG - 1}"), ev_modmm[F - 1]) if ck < 2 else ()
        w1 = (ev_wcs[ck - 2],) if ck >= 2 else ()
        ev_wc[ck] = P.dma("g", lambda e, ck=ck, par=par: e.dma_start(
            out=arv(WCST + par * D, D), in_=wo[128 * ck:128 * (ck + 1), :]),
            waits=w0 + w1, inc=f"wcl{par}")
        ev_wcs[ck] = P.dma("sc", lambda e, ck=ck, par=par: e.dma_start(
            out=wo_bf[128 * ck:128 * (ck + 1), :], in_=arv(WCST + par * D, D)),
            waits=(ev_wc[ck],), inc=f"wcs{par}")
    ev_wcst = (P.ev("wcs0"), P.ev("wcs1"))

    def columnize_gate(which, dst, scale):
        prev = None
        nb = D // cfg.BB
        for j in range(nb):
            col = cfg.BB * j
            r, loc = GRP * which + col // cfg.GASH, col % cfg.GASH
            evm = P.op("t", lambda e, r=r, loc=loc: e.matmul(
                pM[:, 0:cfg.BB], bselS[:, 128 * r:128 * (r + 1)],
                modg[:, loc:loc + cfg.BB], start=True, stop=True),
                waits=(ev_garb, RES["pM"], prev), inc="gcm")
            prev = P.op("v", lambda e, j=j: e.tensor_copy(
                arf(GAB, D)[:, cfg.BB * j:cfg.BB * (j + 1)], pM[:, 0:cfg.BB]),
                waits=(evm,), inc="gcv")
            RES["pM"] = prev
        evt = None
        for fi in range(F):
            evt = P.op("t", lambda e, fi=fi: e.transpose(
                pM[:, fi:fi + 1], arf(GAB, D)[0:1, 128 * fi:128 * (fi + 1)],
                identfS[0:1, 0:1]),
                waits=(prev, ev_cload, RES["pM"] if fi == 0 else None),
                inc="gct" if fi == F - 1 else None)
        ev = P.op("v", lambda e: e.tensor_copy(sm[:, 0:F], pM[:, 0:F]), waits=(evt,), inc="gcv")
        RES["pM"] = ev
        ev = P.op("v", lambda e: e.tensor_scalar(sm2[:, 0:F], sm[:, 0:F], 0.0, 2.0, ALU.is_ge, ALU.mult),
                  waits=(ev,), inc="gcv")
        ev = P.op("v", lambda e: e.tensor_scalar(sm2[:, 0:F], sm2[:, 0:F], -1.0, None, ALU.add),
                  waits=(ev,), inc="gcv")
        ev = P.op("v", lambda e: e.tensor_scalar(sm[:, F:2 * F], sm[:, 0:F], -1.0, None, ALU.mult),
                  waits=(ev,), inc="gcv")
        ev = P.op("v", lambda e: e.tensor_tensor(sm[:, 0:F], sm[:, 0:F], sm[:, F:2 * F], ALU.max),
                  waits=(ev,), inc="gcv")
        ev = P.op("v", lambda e: e.tensor_scalar(sm[:, 0:F], sm[:, 0:F], 0.25, float(scale), ALU.max, ALU.mult),
                  waits=(ev,), inc="gcv")
        ev = P.op("v", lambda e, dst=dst: e.tensor_tensor(dst[:, 0:F], sm[:, 0:F], sm2[:, 0:F], ALU.mult),
                  waits=(ev,), inc="gcv")
        return ev

    ev_ga = columnize_gate(0, gacol, 1.0)
    ev_gb = columnize_gate(1, gb4col, LORA_SCALE)

    ev_cbpb = bcast_gate(4, capb, True, extra_wait=ev_s1_last)
    ev_sbb = bcast_gate(3, sab, False, extra_wait=ev_s1_last)

    wol_ev = {}

    def emit_wol(gi, wait):
        par = gi % 2
        og, ocg = divmod(gi, ND512)
        evg = evs = None
        for fi in range(F):
            eng = "g" if fi % 2 == 0 else "sy"
            ev = P.dma(eng, lambda e, ocg=ocg, fi=fi, par=par: e.dma_start(
                out=arv(WOSL + par * F * 512 + fi * 512, 512),
                in_=wo_bf[128 * fi:128 * (fi + 1), 512 * ocg:512 * (ocg + 1)]),
                waits=wait + ev_wcst,
                inc=(f"wolg{par}" if fi % 2 == 0 else f"wols{par}"))
            if fi % 2 == 0:
                evg = ev
            else:
                evs = ev
        wol_ev[gi] = (evg, evs)

    emit_wol(0, (ev_Wmm["v"], ev_spw0, ev_spw1))
    emit_wol(1, (ev_Wmm["v"], ev_spw0, ev_spw1))

    sc_banks = [pA, pB, pC]
    pv_banks = [pD, pM]
    sm_banks = [pE, pF]
    pvname = ["pD", "pM"]
    smname = ["pE", "pF"]
    bankres = {"pD": ev_modev_all, "pM": RES["pM"], "pE": ev_gaev, "pF": ev_gaev}
    exp_by_bank = {0: st_rot.get(("evac", 0)), 1: st_rot.get(("evac", 1)),
                   2: ev_modev_all}
    items = [(qg, h) for qg in range(NQG) for h in range(HPC)]
    flat = []
    for ii, (qg, h) in enumerate(items):
        nkb = (qg + 1) * TOKG // 128
        for kb in range(nkb):
            flat.append((ii, qg, h, kb, nkb))
    NFL = len(flat)
    ev_kl, ev_vl, ev_ql = {}, {}, {}
    prev_kv = {}
    prev_store = {}
    slot_busy = {}
    probs_ev = {}
    smm_ev = {}
    pv_ev = {}
    ev_aga = {}

    def emit_loads(ii):
        qg, h = items[ii]
        L = (qg + 1) * TOKG
        par = ii % 2
        gate = (ev_spw0, ev_spw1, prev_kv.get(par))
        ev_kl[ii] = P.dma("sy", lambda e, h=h, L=L, o=KBUF + par * S: e.dma_start(
            out=arv(o, L), in_=k_sp[h, :, 0:L]), waits=gate, inc=f"kl{par}")
        ev_vl[ii] = P.dma("sy", lambda e, h=h, L=L, o=VBUF + par * S: e.dma_start(
            out=arv(o, L), in_=v_sp[h, :, 0:L]), waits=gate, inc=f"vl{par}")
        ev_ql[ii] = P.dma("sy", lambda e, h=h, qg=qg, o=QBUF + par * 512: e.dma_start(
            out=arv(o, 512), in_=q_sp[h, :, TOKG * qg:TOKG * (qg + 1)]),
            waits=gate, inc=f"ql{par}")

    def emit_scores(j):
        ii, qg, h, kb, nkb = flat[j]
        par = ii % 2
        bk = sc_banks[j % 3]
        slot = PRB + 512 * (j % 4)
        ev = P.op("t", lambda e, bk=bk, kb=kb, par=par: e.matmul(
            bk[:, :], arv(KBUF + par * S + 128 * kb, 128),
            arv(QBUF + par * 512, 512), start=True, stop=True),
            waits=(ev_kl[ii], ev_ql[ii], exp_by_bank.get(j % 3)), inc="scm")
        ev_e = P.op("sc", lambda e, bk=bk, slot=slot: e.activation(
            arv(slot, 512), bk[:, :], AF.Exp, bias=negC[:, :],
            scale=1.0 / math.sqrt(HD)),
            waits=(ev, ev_negC, slot_busy.get(j % 4)), inc="exp")
        exp_by_bank[j % 3] = ev_e
        dj = kb - 4 * qg
        if dj >= 0:
            ev_e = P.op("v", lambda e, slot=slot, dj=dj: e.tensor_tensor(
                arv(slot, 512), arv(slot, 512),
                masksS[:, TOKG * dj:TOKG * (dj + 1)], ALU.mult),
                waits=(ev_e, ev_cload), inc="msk")
        probs_ev[j] = ev_e

    def emit_mmpv(j):
        ii, qg, h, kb, nkb = flat[j]
        par = ii % 2
        slot = PRB + 512 * (j % 4)
        smbk = sm_banks[ii % 2]
        pvbk = pv_banks[ii % 2]
        w0 = (bankres[smname[ii % 2]],) if kb == 0 else ()
        ev_m = P.op("t", lambda e, smbk=smbk, slot=slot, kb=kb, nkb=nkb: e.matmul(
            smbk[:, :], ones128S[:, :], arv(slot, 512),
            start=(kb == 0), stop=(kb == nkb - 1)),
            waits=(probs_ev[j], ev_cload) + w0,
            inc="smm" if kb == nkb - 1 else None)
        w1 = (bankres[pvname[ii % 2]],) if kb == 0 else ()
        ev_p = P.op("t", lambda e, pvbk=pvbk, slot=slot, kb=kb, nkb=nkb, par=par: e.matmul(
            pvbk[:, :], arv(VBUF + par * S + 128 * kb, 128), arv(slot, 512),
            start=(kb == 0), stop=(kb == nkb - 1)),
            waits=(probs_ev[j], ev_vl[ii]) + w1, inc="pvm")
        slot_busy[j % 4] = ev_p
        if kb == nkb - 1:
            smm_ev[ii] = ev_m
            pv_ev[ii] = ev_p

    def emit_it_tail(ii):
        qg, h = items[ii]
        par = ii % 2
        smbk = sm_banks[ii % 2]
        pvbk = pv_banks[ii % 2]
        ev_rc = P.op("v", lambda e, smbk=smbk: e.reciprocal(sm[:, :], smbk[:, :]),
                     waits=(smm_ev[ii],), inc="rcp")
        bankres[smname[ii % 2]] = ev_rc
        ev_div = P.op("v", lambda e, pvbk=pvbk, par=par: e.tensor_tensor(
            arv(ATTS + 512 * par, 512), pvbk[:, :], sm[:, :], ALU.mult),
            waits=(ev_rc, pv_ev[ii], prev_store.get(par)), inc="div")
        bankres[pvname[ii % 2]] = ev_div
        prev_store[par] = P.dma("sy", lambda e, qg=qg, h=h, par=par: e.dma_start(
            out=aga_i.ap()[qg, 128 * h:128 * (h + 1), :],
            in_=arv(ATTS + 512 * par, 512)),
            waits=(ev_div,), inc=f"aw{qg}")
        prev_kv[par] = pv_ev[ii]
        if h == HPC - 1:
            ev_aga[qg] = P.op("g", lambda e, qg=qg: e.collective_compute(
                "AllGather", ALU.bypass,
                replica_groups=[list(range(GRP)), list(range(GRP, 2 * GRP))],
                ins=[aga_i.ap()[qg].opt()], outs=[aga_o.ap()[qg].opt()]),
                waits=(P.ev(f"aw{qg}"),), inc="aga", amt=1, kind="d")
        if ii + 2 < len(items):
            emit_loads(ii + 2)

    emit_loads(0)
    emit_loads(1)
    emit_scores(0)
    emit_scores(1)
    for j in range(NFL):
        if j + 2 < NFL:
            emit_scores(j + 2)
        emit_mmpv(j)
        ii, qg, h, kb, nkb = flat[j]
        if kb == nkb - 1:
            emit_it_tail(ii)
    ev_aga_all = P.ev("aga")
    RES["pM"] = bankres["pM"]
    attn_pD = bankres["pD"]
    attn_pE = bankres["pE"]

    aga_flat = aga_o.ap().rearrange("a b c -> (a b) c")
    ev_woc_prev = None
    fill_hist = {}
    ev_x2w = []
    ev_attg_free = None
    ev_ep_pt = st_rot.get("ptv")
    for og in range(NOG):
        ev_gath = None
        for fi in range(F):
            ev_gath = P.op("g", lambda e, og=og, fi=fi: e.indirect_dma_start(
                arv(ATTG + fi * 512, 512), None, aga_flat,
                IndirectOffsetOnAxis(ap=idxb[:, og * F + fi:og * F + fi + 1], axis=0)),
                waits=(ev_aga_all, ev_idx, ev_attg_free), inc="agard", amt=16, kind="d")
        for ocg in range(ND512):
            gi = og * ND512 + ocg
            par = gi % 2
            if gi not in wol_ev:
                emit_wol(gi, (fill_hist.get(gi - 2),))
            ev_wg, ev_ws = wol_ev[gi]
            ev_fill = {}
            for oc in range(4):
                bk = banks6[oc]
                evf = None
                for fi in range(F):
                    evf = P.op("t", lambda e, fi=fi, oc=oc, par=par, bk=bk: e.matmul(
                        bk[:, :], arv(WOSL + par * F * 512 + fi * 512 + 128 * oc, 128),
                        arv(ATTG + fi * 512, 512),
                        start=(fi == 0), stop=(fi == F - 1)),
                        waits=(ev_wg, ev_ws, ev_gath, ev_x2w[-1] if ev_x2w else None),
                        inc="wocm" if fi == F - 1 else None)
                ev_fill[oc] = evf
            ev_woc_prev = ev_fill[3]
            fill_hist[gi] = ev_fill[3]
            ev_tt = {}
            for oc in range(4):
                gidx = ocg * 4 + oc
                e0 = P.op("v", lambda e, oc=oc, gidx=gidx: e.tensor_scalar(
                    arv(X2STG + 2048 + 512 * (oc % 2), 512), banks6[oc][:, :],
                    gacol[:, gidx:gidx + 1], None, ALU.mult),
                    waits=(ev_fill[oc], ev_ga, ev_tt.get(oc - 2)) + ev_wcst,
                    inc="epv")
                evt = None
                for j in range(4):
                    evt = P.op("t", lambda e, j=j, oc=oc: e.transpose(
                        pT[:, 128 * j:128 * (j + 1)],
                        arv(X2STG + 2048 + 512 * (oc % 2), 512)[:, 128 * j:128 * (j + 1)],
                        identS[:, :]),
                        waits=(e0, ev_ep_pt), inc="eptr" if j == 3 else None)
                exl = P.dma("sy", lambda e, og=og, gidx=gidx, oc=oc: e.dma_start(
                    out=arf(XS + 512 * (oc % 2), 512).rearrange("p (j c) -> p j c", j=4),
                    in_=xown[og * 512:(og + 1) * 512, 128 * gidx:128 * (gidx + 1)]
                    .rearrange("(j p) c -> p j c", p=128)),
                    waits=(ev_tt.get(oc - 2),), inc=f"exl{oc % 2}")
                ev2 = P.op("v", lambda e, evt=evt, oc=oc: e.tensor_tensor(
                    arv(X2STG + 512 * oc, 512), pT[:, :],
                    arf(XS + 512 * (oc % 2), 512), ALU.add),
                    waits=(evt, exl, ev_x2w[-1] if ev_x2w else None), inc="x2v")
                ev_ep_pt = ev2
                ev_tt[oc] = ev2
            for j in range(4):
                ev_x2w.append(P.dma("sy", lambda e, ocg=ocg, og=og, j=j: e.dma_start(
                    out=x2_hb[og * 512 + 128 * j:og * 512 + 128 * (j + 1),
                              512 * ocg:512 * (ocg + 1)]
                    .rearrange("p (oc c) -> p oc c", oc=4),
                    in_=arv(X2STG, 2048).rearrange("p (oc j c) -> p j oc c", oc=4, j=4)[:, j]),
                    waits=(ev_tt[3],), inc="x2w"))
        ev_attg_free = ev_fill[3]
    ev_x2done = P.ev("x2w")

    GBBT = 0
    NGR = 2 * NC
    mgc = arf(GAB, cfg.GASH)
    ev_c0 = P.op("v", lambda e: e.tensor_scalar(
        mgc[0:NGR, :], modg[:, :], 0.0, 2.0, ALU.is_ge, ALU.mult),
        waits=(ev_gb, ev_woc_prev), inc="gcv")
    ev_c0 = P.op("v", lambda e: e.tensor_scalar(
        mgc[0:NGR, :], mgc[0:NGR, :], -1.0, None, ALU.add),
        waits=(ev_c0,), inc="gcv")
    ev_cl = P.op("v", lambda e: e.tensor_tensor(
        modg[:, :], modg[:, :], mgc[0:NGR, :], ALU.mult),
        waits=(ev_c0,), inc="gcv")
    ev_cl = P.op("v", lambda e: e.tensor_scalar(
        modg[:, :], modg[:, :], 0.25, None, ALU.max),
        waits=(ev_cl,), inc="gcv")
    ev_cl = P.op("v", lambda e: e.tensor_tensor(
        modg[:, :], modg[:, :], mgc[0:NGR, :], ALU.mult),
        waits=(ev_cl,), inc="gcv")
    ev_gbb = None
    for j in range(D // 512):
        col = 512 * j
        r, loc = GRP + col // cfg.GASH, col % cfg.GASH
        evm = P.op("t", lambda e, r=r, loc=loc: e.matmul(
            pM[:, 0:512], bselS[:, 128 * r:128 * (r + 1)],
            modg[:, loc:loc + 512], start=True, stop=True),
            waits=(ev_cl, RES["pM"], ev_gbb), inc="gbm")
        ev_gbb = P.op("v", lambda e, j=j: e.tensor_scalar(
            arv(GBBT + 512 * j, 512), pM[:, 0:512], LORA_SCALE, None,
            ALU.mult), waits=(evm,), inc="gcv")
        RES["pM"] = ev_gbb

    ev_ffnw = None
    for fi in range(F):
        ev_ffnw = P.dma("g", lambda e, fi=fi: e.dma_start(
            out=arv(FFNW + fi * 64, 64), in_=Apk[128 * fi:128 * (fi + 1), :]),
            waits=(ev_woc_prev,), inc="ffnw")
        ev_ffnw = P.dma("g", lambda e, fi=fi: e.dma_start(
            out=arv(FFNW + F * 64 + fi * 4, 4), in_=routerT[128 * fi:128 * (fi + 1), :]),
            waits=(ev_woc_prev,), inc="ffnw")
    ev_ffnw = P.dma("g", lambda e: e.dma_start(out=BpkS[0:64, :], in_=Bpk),
                    waits=(ev_woc_prev,), inc="ffnw")
    ev_ffnw = P.ev("ffnw")

    ev_ffnd = {}
    ev_h2c_prev = ev_ep_pt
    ag_hist = {}
    ap_hist = {}
    ev_h2r_free = None
    ev_mid_prev = None
    ev_gex_prev = None
    ev_out_prev = None
    del_bank = {}
    del_bank0 = {0: exp_by_bank[2], 1: bankres["pF"]}
    for tt in range(cfg.NOT):
        par = tt % 2
        o = 6 * ND512 * par
        om = 8 * par
        ev_x2 = P.dma("sy", lambda e, tt=tt, par=par: e.dma_start(
            out=arv(X2T + par * D, D), in_=x2_hb[128 * tt:128 * (tt + 1), :]),
            waits=(ev_x2done, ev_ffnd.get(tt - 2)), inc=f"x2l{par}")
        ev_bs = None
        for i in range(ND512):
            ev_bs = P.op("v", lambda e, i=i, par=par, o=o: e.bn_stats(
                stats[:, o + 6 * i:o + 6 * i + 6],
                arv(X2T + par * D, D)[:, 512 * i:512 * (i + 1)]),
                waits=(ev_x2, ag_hist.get(par) if i == 0 else None), inc="l2v")
        ev_ag = P.op("v", lambda e, o=o, om=om: e.bn_aggr(
            mv[:, om:om + 2], stats[:, o:o + 6 * ND512]),
            waits=(ev_bs, ap_hist.get(par)), inc="l2v")
        ag_hist[par] = ev_ag
        ev_ep = P.op("v", lambda e, om=om: e.tensor_scalar(
            mv[:, om + 2:om + 3], mv[:, om + 1:om + 2], LN_EPS, None, ALU.add),
            waits=(ev_ag,), inc="l2v")
        ev_v = P.op("v", lambda e, om=om: e.tensor_scalar(
            mv[:, om + 3:om + 4], mv[:, om:om + 1], -1.0, None, ALU.mult),
            waits=(ev_ag,), inc="l2v")
        ev_sq = P.op("sc", lambda e, om=om: e.activation(
            mv[:, om + 4:om + 5], mv[:, om + 2:om + 3], AF.Sqrt),
            waits=(ev_v, ev_ep), inc="l2sq")
        ev_r = P.op("v", lambda e, om=om: e.reciprocal(
            mv[:, om + 5:om + 6], mv[:, om + 4:om + 5]), waits=(ev_sq,), inc="l2v")
        ev_b = P.op("v", lambda e, om=om: e.tensor_tensor(
            mv[:, om + 6:om + 7], mv[:, om + 3:om + 4], mv[:, om + 5:om + 6],
            ALU.mult), waits=(ev_r,), inc="l2v")
        ev1 = P.op("sc", lambda e, par=par, om=om: e.activation(
            arv(H2R, D), arv(X2T + par * D, D), AF.Identity,
            bias=mv[:, om + 6:om + 7], scale=mv[:, om + 5:om + 6]),
            waits=(ev_b, ev_h2r_free), inc="l2ap")
        ap_hist[par] = ev1
        ev2 = P.op("v", lambda e: e.tensor_tensor(arv(H2R, D), arv(H2R, D), capb[:, :], ALU.mult),
                   waits=(ev1, ev_cbpb), inc="l2v")
        ev3 = P.op("v", lambda e: e.tensor_tensor(arv(H2R, D), arv(H2R, D), sab[:, :], ALU.add),
                   waits=(ev2, ev_sbb), inc="l2v")
        ev_h2 = None
        for g4 in range(F // 4):
            evt = None
            for j in range(4):
                fi = g4 * 4 + j
                evt = P.op("t", lambda e, fi=fi, j=j: e.transpose(
                    pT[:, 128 * j:128 * (j + 1)], arv(H2R, D)[:, 128 * fi:128 * (fi + 1)],
                    identS[:, :]),
                    waits=(ev3, ev_h2c_prev), inc="h2t" if j == 3 else None)
            ev_h2 = P.op("sc", lambda e, g4=g4: e.activation(
                arv(H2C + 512 * g4, 512), pT[:, :], AF.Copy),
                waits=(evt, ev_mid_prev), inc="h2c")
            ev_h2c_prev = ev_h2
            ev_h2r_free = evt

        def h2chunk(fi):
            return arv(H2C + 128 * fi, 128)

        evl = None
        for fi in range(F):
            evl = P.op("t", lambda e, fi=fi: e.matmul(
                pM[0:4, 0:128], arv(FFNW + F * 64 + fi * 4, 4), h2chunk(fi),
                start=(fi == 0), stop=(fi == F - 1)),
                waits=(ev_h2, ev_ffnw, RES["pM"] if fi == 0 else None),
                inc="lgm" if fi == F - 1 else None)
        ev_lg = P.op("v", lambda e: e.tensor_copy(gat[0:4, :], pM[0:4, 0:128]),
                     waits=(evl,), inc="lgv")
        ev_lt = P.op("t", lambda e: e.transpose(pM[:, 8:12], gat[0:4, :], identfS[0:4, 0:4]),
                     waits=(ev_lg,), inc="lgt")
        ev = P.op("v", lambda e: e.tensor_copy(lgt[:, 0:4], pM[:, 8:12]), waits=(ev_lt,), inc="gv")
        RES["pM"] = ev
        ev = P.op("v", lambda e: e.tensor_tensor(lgt[:, 4:5], lgt[:, 0:1], lgt[:, 1:2], ALU.max), waits=(ev,), inc="gv")
        ev = P.op("v", lambda e: e.tensor_tensor(lgt[:, 6:7], lgt[:, 2:3], lgt[:, 3:4], ALU.max), waits=(ev,), inc="gv")
        ev = P.op("v", lambda e: e.tensor_tensor(lgt[:, 5:6], lgt[:, 0:1], lgt[:, 1:2], ALU.min), waits=(ev,), inc="gv")
        ev = P.op("v", lambda e: e.tensor_tensor(lgt[:, 7:8], lgt[:, 2:3], lgt[:, 3:4], ALU.min), waits=(ev,), inc="gv")
        ev = P.op("v", lambda e: e.tensor_tensor(lgt[:, 8:9], lgt[:, 4:5], lgt[:, 6:7], ALU.max), waits=(ev,), inc="gv")
        ev = P.op("v", lambda e: e.tensor_tensor(lgt[:, 9:10], lgt[:, 4:5], lgt[:, 6:7], ALU.min), waits=(ev,), inc="gv")
        ev = P.op("v", lambda e: e.tensor_tensor(lgt[:, 10:11], lgt[:, 5:6], lgt[:, 7:8], ALU.max), waits=(ev,), inc="gv")
        ev = P.op("v", lambda e: e.tensor_tensor(lgt[:, 11:12], lgt[:, 9:10], lgt[:, 10:11], ALU.max), waits=(ev,), inc="gv")
        ev = P.op("v", lambda e, om=om: e.tensor_scalar(
            mv[:, om + 6:om + 7], lgt[:, 8:9], -1.0, None, ALU.mult),
            waits=(ev,), inc="gv")
        ev_e = P.op("sc", lambda e, om=om: e.activation(
            lgt[:, 12:16], lgt[:, 0:4], AF.Exp, bias=mv[:, om + 6:om + 7]),
            waits=(ev,), inc="gse")
        ev = P.op("v", lambda e: e.tensor_scalar(lgt[:, 0:4], lgt[:, 0:4], lgt[:, 11:12], None, ALU.is_ge),
                  waits=(ev_e,), inc="gv")
        ev = P.op("v", lambda e: e.tensor_tensor(lgt[:, 12:16], lgt[:, 12:16], lgt[:, 0:4], ALU.mult),
                  waits=(ev,), inc="gv")
        ev = P.op("v", lambda e, om=om: e.reduce_sum(
            mv[:, om + 7:om + 8], lgt[:, 12:16], axis=AX.X), waits=(ev,), inc="gv")
        ev = P.op("v", lambda e, om=om: e.reciprocal(
            mv[:, om + 6:om + 7], mv[:, om + 7:om + 8]), waits=(ev,), inc="gv")
        ev = P.op("v", lambda e, om=om: e.tensor_scalar(
            lgt[:, 12:16], lgt[:, 12:16], mv[:, om + 6:om + 7], None, ALU.mult),
            waits=(ev,), inc="gv")
        ev_gt = P.op("t", lambda e: e.transpose(pM[0:4, 16:144], lgt[:, 12:16], identfS[:, :]),
                     waits=(ev, RES["pM"]), inc="gtt")
        ev_gt = P.op("v", lambda e: e.tensor_copy(gat[0:4, :], pM[0:4, 16:144]), waits=(ev_gt,), inc="gv")
        RES["pM"] = ev_gt
        ev_ge = P.op("t", lambda e: e.matmul(pE[0:64, 0:128], expandS[:, :], gat[0:4, :],
                                             start=True, stop=True),
                     waits=(ev_gt, ev_cload, ev_gex_prev, attn_pE), inc="gexm")
        ev_ge = P.op("v", lambda e: e.tensor_copy(sm[0:64, 0:128], pE[0:64, 0:128]),
                     waits=(ev_ge,), inc="gexv")
        ev_gex_prev = ev_ge
        evm = None
        for fi in range(F):
            evm = P.op("t", lambda e, fi=fi: e.matmul(
                pD[0:64, 0:128], arv(FFNW + fi * 64, 64), h2chunk(fi),
                start=(fi == 0), stop=(fi == F - 1)),
                waits=(ev_h2, ev_ffnw, ev_mid_prev, attn_pD),
                inc="midm" if fi == F - 1 else None)
        ev_mid = P.op("v", lambda e: e.tensor_tensor(
            arv(MIDB, 128)[0:64, :], pD[0:64, 0:128], sm[0:64, 0:128], ALU.mult),
            waits=(evm, ev_ge), inc="midv")
        ev_mid_prev = ev_mid
        for j in range(ND512):
            bank = pC if j % 2 == 0 else pF
            w0 = (del_bank.get(j % 2, del_bank0[j % 2]),)
            ev_d = P.op("t", lambda e, j=j, bank=bank: e.matmul(
                bank[:, :], arv(MIDB, 128)[0:64, :],
                BpkS[0:64, 512 * j:512 * (j + 1)],
                start=True, stop=True),
                waits=(ev_mid, ev_ffnw) + w0, inc="dm")
            ev_d2 = P.op("v", lambda e, j=j, bank=bank: e.tensor_tensor(
                arf(OUTS, D)[:, 512 * j:512 * (j + 1)], bank[:, :],
                arv(GBBT + 512 * j, 512), ALU.mult),
                waits=(ev_d, ev_gbb) + ((P.ev("outw"),) if j == 0 else ()),
                inc="dts")
            del_bank[j % 2] = ev_d2
            ev_d4 = P.op("v", lambda e, j=j, par=par: e.tensor_tensor(
                arf(OUTS, D)[:, 512 * j:512 * (j + 1)],
                arf(OUTS, D)[:, 512 * j:512 * (j + 1)],
                arv(X2T + par * D, D)[:, 512 * j:512 * (j + 1)], ALU.add),
                waits=(ev_d2, ev_x2), inc="outv")
            ev_out_prev = ev_d4
        ev_ffnd[tt] = ev_out_prev
        P.dma("sy", lambda e, tt=tt: e.dma_start(
            out=OUT[128 * tt:128 * (tt + 1), :], in_=arf(OUTS, D)),
            waits=(ev_out_prev,), inc="outw")

    P.op("sy", None, waits=(P.ev("outw"),))
    P.emit(nc, ctx)
    return nc, ctx


_BUILD_CACHE = {}


def get_program(key):
    if key not in _BUILD_CACHE:
        cfg = CFG(*key)
        nc, _ = build(cfg)
        _BUILD_CACHE[key] = (nc, cfg)
    return _BUILD_CACHE[key]


def kernel(L0_attn_hidden_states, L0_attn_wq, L0_attn_wk, L0_attn_wv,
           L0_attn_wo, L0_attn_kq1, L0_attn_kq2, L0_attn_kk1, L0_attn_kk2,
           L0_attn_scalingD, L0_attn_interaction_k, L0_attn_scalingD2,
           L0_ffn_router_w, L0_ffn_A, L0_ffn_B, L0_c, L0_wmod):
    x = np.asarray(L0_attn_hidden_states, dtype=np.float32)
    B, S, D = x.shape
    nc, cfg = get_program((D, S))
    T = dict(x=x, wq=np.asarray(L0_attn_wq, np.float32),
             wk=np.asarray(L0_attn_wk, np.float32),
             wv=np.asarray(L0_attn_wv, np.float32),
             wo=np.asarray(L0_attn_wo, np.float32),
             wmod=np.asarray(L0_wmod, np.float32),
             c=np.asarray(L0_c, np.float32),
             router_w=np.asarray(L0_ffn_router_w, np.float32),
             A=np.asarray(L0_ffn_A, np.float32),
             B=np.asarray(L0_ffn_B, np.float32))
    consts = host_constants(cfg)
    in_maps = [host_core_inputs(cfg, c_, T, consts) for c_ in range(NC)]
    res = run_bass_kernel_spmd(nc, in_maps, core_ids=list(range(NC)))
    out = np.empty((B, S, D), dtype=np.float32)
    for c_ in range(NC):
        b, p = c_ // GRP, c_ % GRP
        out[b, cfg.OWN * p: cfg.OWN * (p + 1)] = res.results[c_]["out"]
    return out
